# version 13
# speedup vs baseline: 1.2811x; 1.0097x over previous
import sys

sys.path.insert(0, "/opt/trn_rl_repo")

import numpy as np
import concourse.bass as bass
import concourse.mybir as mybir
from concourse.bass_utils import run_bass_kernel_spmd

F32 = mybir.dt.float32
BF16 = mybir.dt.bfloat16
AF = mybir.ActivationFunctionType
ALU = mybir.AluOpType
AX = mybir.AxisListType

NCORES = 8
B, S, D = 4, 2048, 2048
H, DH = 16, 128
HL = H // 2
TOK = S // 2
HID = 4 * D
INV_SQRT_DH = float(1.0 / np.sqrt(np.float32(DH)))
LN_EPS = 1e-5
RMS_EPS = float(np.finfo(np.float32).eps)
THETA = 0.5 + 2.0 ** -10

ENGS = ("tensor", "vector", "scalar", "gpsimd", "sync")


class Em:

    def __init__(self, sems):
        self.sems = sems
        self.q = {e: [] for e in ENGS}
        self.cnt = {k: 0 for k in sems}
        self._lastwait = {}

    def op(self, eng, fn, inc=None, amt=1):
        if inc is None:
            self.q[eng].append(fn)
            return None
        self.cnt[inc] += amt
        sem = self.sems[inc]
        self.q[eng].append(lambda E, f=fn, s=sem, a=amt: f(E).then_inc(s, a))
        return self.cnt[inc]

    def wait(self, eng, sem_name, val):
        if not val:
            return
        key = (eng, sem_name)
        if self._lastwait.get(key, 0) >= val:
            return
        assert val <= self.cnt[sem_name], \
            f"wait {sem_name}>={val} but only {self.cnt[sem_name]} emitted"
        self._lastwait[key] = val
        s = self.sems[sem_name]
        self.q[eng].append(lambda E, s=s, v=val: E.wait_ge(s, v))

    def fence(self, engs=ENGS, exclude=("ccgu", "ccp", "ccrs", "dgp", "dq0", "dq1",
                                         "dwqk0", "dwqk1", "dgu0", "dgu1",
                                         "dwp0", "dwp1", "dwv", "dx0", "dx1")):
        snap = dict(self.cnt)
        for e in engs:
            for k, v in snap.items():
                if v and k not in exclude:
                    self.wait(e, k, v)


DBG_NO_V = False
DBG_NO_QUANT = False
DBG_LN_TILES = 16


def build(stage="full", taps=()):
    nc = bass.Bass("TRN2", target_bir_lowering=False)

    x_full = nc.dram_tensor("x_full", [S, D], F32, kind="ExternalInput")
    x_res = nc.dram_tensor("x_res", [TOK, D], F32, kind="ExternalInput")
    wqk = nc.dram_tensor("wqk", [2, HL, 16, 128, 128], F32, kind="ExternalInput")
    wv = nc.dram_tensor("wv", [16, 128, HL * 128], F32, kind="ExternalInput")
    wo = nc.dram_tensor("wo", [HL, 128, D], F32, kind="ExternalInput")
    wgu = nc.dram_tensor("wgu", [16, 128, 2048], F32, kind="ExternalInput")
    wp = nc.dram_tensor("wp", [8, 128, 2048], F32, kind="ExternalInput")
    out_t = nc.dram_tensor("out", [TOK, D], F32, kind="ExternalOutput")

    cc_gu_in = nc.dram_tensor("cc_gu_in", [2048 * 16, 128], BF16)
    cc_gu_out = nc.dram_tensor("cc_gu_out", [NCORES * 2048 * 16, 128], BF16,
                               addr_space="Shared")
    cc_p_in = nc.dram_tensor("cc_p_in", [4096, 512], BF16)
    cc_p_out = nc.dram_tensor("cc_p_out", [NCORES * 4096, 512], BF16,
                              addr_space="Shared")
    wo_part = nc.dram_tensor("wo_part", [S, D], F32)
    attn_out = nc.dram_tensor("attn_out", [TOK, D], F32)
    x1d = nc.dram_tensor("x1d", [TOK, D], F32)

    from contextlib import ExitStack
    stack = ExitStack()
    
    ps = lambda name, shape, dt: stack.enter_context(nc.psum_tensor(name, shape, dt))

    _open_cms = {}
    reg = {}

    def sb_open(name, shape, dt, side=None):
        cm = nc.sbuf_tensor(name, shape, dt, side=side) if side else nc.sbuf_tensor(name, shape, dt)
        h = cm.__enter__()
        _open_cms[name] = cm
        reg[name] = h
        return h

    def sb_close(name):
        _open_cms.pop(name).__exit__(None, None, None)

    def sbr(name, shape, dt):
        h = stack.enter_context(nc.sbuf_tensor(name, shape, dt))
        reg[name] = h
        return h

    with stack:
        ident = sbr("ident", [128, 128], BF16)
        masks = sbr("masks", [128, 4 * 512], BF16)
        bones = sbr("bones", [128, 128], BF16)
        ones_col = sbr("ones_col", [128, 1], BF16)
        ones128 = sbr("ones128", [128, 128], BF16)
        ones128f = sbr("ones128f", [128, 128], F32)
        ones_col_f = sbr("ones_col_f", [128, 1], F32)
        epsb = sbr("epsb", [128, 2], F32)
        ones_row = sbr("ones_row", [1, 128], F32)
        stats = sbr("stats", [128, 16], F32)
        srow = sbr("srow", [128, 1024], F32)
        bias_h = sbr("bias_h", [128, 2], F32)
        u_sb = sbr("u_sb", [128, 3 * 512], BF16)
        kq_sb = sbr("kq_sb", [128, 2 * 2048], BF16)
        invs_sb = sbr("invs_sb", [128, 512], F32)

        psA = ps("psA", [128, 512], F32)
        psB = ps("psB", [128, 512], F32)
        psC = ps("psC", [128, 512], F32)
        psD = ps("psD", [128, 512], F32)
        psE = ps("psE", [128, 512], F32)
        psF = ps("psF", [128, 512], F32)
        psG = ps("psG", [128, 512], F32)
        psT = ps("psT", [128, 1024], BF16)

        qwbf = qabs = qsgn = qg = qsbf = None

        v_sb = sb_open("v_sb", [128, 16 * 1024], BF16)
        xn1T = sb_open("xn1T", [128, 16 * 2048], BF16)
        z_sb = None
        xin = sb_open("xin", [128, 3 * 2048], F32)
        sqx = sb_open("sqx", [128, 2048], F32)
        wv_sb = sb_open("wv_sb", [128, 16 * 1024], BF16)
        xnbuf = sb_open("xnbuf", [128, 3 * 2048], BF16)

        sems = {}
        for name in ("pe", "dve", "act", "pool", "dsy", "dgp", "ccgu", "ccp", "ccrs",
                     "dx0", "dx1", "dx2", "dwv", "dwqk0", "dwqk1", "dq0", "dq1",
                     "dgu0", "dgu1", "dwp0", "dwp1", "dwo0", "dwo1"):
            sems[name] = stack.enter_context(nc.semaphore(name))
        em = Em(sems)
        st8 = {}

        def emit_consts():
            def f(g):
                g.memset(ident[:, :], 1.0)
                g.affine_select(ident[:, :], ident[:, :], [[1, 128]], ALU.is_equal,
                                0.0, base=0, channel_multiplier=-1)
                g.memset(masks[:, :], 1.0)
                for j in range(4):
                    g.affine_select(masks[:, j * 512:(j + 1) * 512],
                                    masks[:, j * 512:(j + 1) * 512],
                                    [[1, 512]], ALU.is_ge, 0.0,
                                    base=-128 * j, channel_multiplier=-1)
                g.memset(bones[:, :], 0.0)
                g.memset(bones[0:64, 0:64], 1.0)
                g.memset(bones[64:128, 64:128], 1.0)
                g.memset(ones_col[:, :], 1.0)
                g.memset(ones128[:, :], 1.0)
                g.memset(ones128f[:, :], 1.0)
                g.memset(ones_col_f[:, :], 1.0)
                g.memset(epsb[:, 0:1], LN_EPS)
                g.memset(epsb[:, 1:2], RMS_EPS)
                return g.memset(ones_row[:, :], 1.0)
            c = em.op("gpsimd", f, inc="pool")
            for e in ("tensor", "vector", "scalar"):
                em.wait(e, "pool", c)

        q_tiles = [("gu", i) for i in range(16)] + [("p", i) for i in range(8)]
        qs = {"done": 0, "issued": 0}
        qpend = []

        def quant_substeps(i):
            kind, idx = q_tiles[i]
            slot = i % 2
            src_t = wgu[idx] if kind == "gu" else wp[idx]
            wb = qwbf[:, slot * 2048:(slot + 1) * 2048]
            ab = qabs[:, slot * 2048:(slot + 1) * 2048]
            sg = qsgn[:, slot * 2048:(slot + 1) * 2048]
            gg = qg[:, slot * 2048:(slot + 1) * 2048]
            steps = []

            def s_load():
                em.wait("gpsimd", "dve", qs.get(("qdve", i - 2, 3), 0))
                def f_ld(g):
                    return g.dma_start(wb, src_t[:, :])
                qs[("ld", i)] = em.op("gpsimd", f_ld, inc="dq%d" % slot, amt=16)
            steps.append(s_load)

            def s_abs():
                em.wait("scalar", "dq%d" % slot, qs[("ld", i)])
                em.wait("scalar", "dve", qs.get(("qdve", i - 2, 3), 0))
                def f_abs(s):
                    return s.activation(ab, wb, AF.Abs)
                qs[("abs", i)] = em.op("scalar", f_abs, inc="act")
                def f_sgn(s):
                    return s.activation(sg, wb, AF.Sign)
                qs[("sgn", i)] = em.op("scalar", f_sgn, inc="act")
            steps.append(s_abs)

            def s_chunk(cs):
                def go():
                    if cs == 0:
                        em.wait("vector", "dgp", qs.get(("out", i - 2), 0))
                    em.wait("tensor", "act", qs[("abs", i)])
                    em.wait("tensor", "dve", qs.get(("sbf", i, cs - 1), 0) or
                            qs.get(("sbf", i - 1, 3), 0))
                    def f_mm(t, cs=cs):
                        return t.matmul(psG[:, :], bones[:, :],
                                        ab[:, cs * 512:(cs + 1) * 512],
                                        start=True, stop=True)
                    p_mm = em.op("tensor", f_mm, inc="pe")
                    em.wait("vector", "pe", p_mm)
                    def f_sbf(v):
                        return v.tensor_scalar(qsbf[:, :], psG[:, :], 1.0 / 64.0, 1e-8,
                                               ALU.mult, ALU.max)
                    d_sbf = em.op("vector", f_sbf, inc="dve")
                    qs[("sbf", i, cs)] = d_sbf
                    em.wait("vector", "dve", d_sbf)
                    em.wait("vector", "act", qs[("sgn", i)])
                    def f_g(v, cs=cs):
                        return v.scalar_tensor_tensor(gg[:, cs * 512:(cs + 1) * 512],
                                                      qsbf[:, :], THETA,
                                                      ab[:, cs * 512:(cs + 1) * 512],
                                                      ALU.mult, ALU.is_lt)
                    em.op("vector", f_g, inc="dve")
                    def f_m(v, cs=cs):
                        sl = gg[:, cs * 512:(cs + 1) * 512]
                        return v.tensor_mul(sl, sl, sg[:, cs * 512:(cs + 1) * 512])
                    qs[("m", i)] = em.op("vector", f_m, inc="dve")
                    def f_wh(v, cs=cs):
                        sl = gg[:, cs * 512:(cs + 1) * 512]
                        return v.tensor_mul(sl, sl, qsbf[:, :])
                    qs[("qdve", i, cs)] = em.op("vector", f_wh, inc="dve")
                return go
            for cs in range(4):
                steps.append(s_chunk(cs))

            def s_out():
                em.wait("gpsimd", "dve", qs[("qdve", i, 3)])
                if kind == "gu":
                    o_ap = bass.AP(cc_gu_in, idx * 128,
                                   [[2048, 128], [262144, 16], [1, 128]])
                    i_ap = bass.AP(qg, slot * 2048, [[2 * 2048, 128], [128, 16], [1, 128]])
                else:
                    o_ap = bass.AP(cc_p_in, idx * 128 * 512,
                                   [[512, 128], [8 * 128 * 512, 4], [1, 512]])
                    i_ap = bass.AP(qg, slot * 2048, [[2 * 2048, 128], [512, 4], [1, 512]])
                def f_out(g):
                    return g.dma_start(o_ap, i_ap)
                qs[("out", i)] = em.op("gpsimd", f_out, inc="dgp", amt=16)
                qs["done"] = i + 1
            steps.append(s_out)
            return steps

        def quant_pump(n):
            for _ in range(n):
                if not qpend and qs["issued"] < len(q_tiles):
                    qpend.extend(quant_substeps(qs["issued"]))
                    qs["issued"] += 1
                if qpend:
                    qpend.pop(0)()

        def quant_step():
            if DBG_NO_QUANT:
                return
            if qs["issued"] < len(q_tiles) or qpend:
                quant_pump(9)

        def quant_gathers():
            em.wait("gpsimd", "dgp", qs[("out", len(q_tiles) - 1)])
            def f_gu(g):
                return g.collective_compute(
                    "AllGather", ALU.bypass, replica_groups=[list(range(NCORES))],
                    ins=[cc_gu_in.ap().opt()], outs=[cc_gu_out.ap().opt()])
            em.op("gpsimd", f_gu, inc="ccgu")
            def f_p(g):
                return g.collective_compute(
                    "AllGather", ALU.bypass, replica_groups=[list(range(NCORES))],
                    ins=[cc_p_in.ap().opt()], outs=[cc_p_out.ap().opt()])
            em.op("gpsimd", f_p, inc="ccp")

        def emit_ln(xtile, outbf, sqbuf, wait_dsy=0, dsem="dsy", pe_war=0):
            em.wait("vector", dsem, wait_dsy)
            def f_s1(v):
                return v.reduce_sum(stats[:, 0:1], xtile, axis=AX.X)
            em.op("vector", f_s1, inc="dve")
            def f_sq(v):
                return v.tensor_mul(sqbuf, xtile, xtile)
            d_sq = em.op("vector", f_sq, inc="dve")
            em.wait("vector", "dve", d_sq)
            def f_s2(v):
                return v.reduce_sum(stats[:, 1:2], sqbuf, axis=AX.X)
            d_s2 = em.op("vector", f_s2, inc="dve")
            st8["sqx_rd"] = d_s2
            def f_mean(v):
                return v.tensor_scalar(stats[:, 2:3], stats[:, 0:1], 1.0 / D, None, ALU.mult)
            d_mean = em.op("vector", f_mean, inc="dve")
            em.wait("vector", "dve", d_mean)
            def f_m2(v):
                return v.tensor_mul(stats[:, 3:4], stats[:, 2:3], stats[:, 2:3])
            d_m2 = em.op("vector", f_m2, inc="dve")
            em.wait("vector", "dve", d_m2)
            def f_var(v):
                return v.scalar_tensor_tensor(stats[:, 4:5], stats[:, 1:2], 1.0 / D,
                                              stats[:, 3:4], ALU.mult, ALU.subtract)
            d_var = em.op("vector", f_var, inc="dve")
            em.wait("scalar", "dve", d_var)
            def f_ln(s):
                return s.activation(stats[:, 5:6], stats[:, 4:5], AF.Ln, bias=epsb[:, 0:1])
            a_ln = em.op("scalar", f_ln, inc="act")
            em.wait("scalar", "act", a_ln)
            def f_ex(s):
                return s.activation(stats[:, 6:7], stats[:, 5:6], AF.Exp, scale=-0.5)
            a_r = em.op("scalar", f_ex, inc="act")
            em.wait("vector", "act", a_r)
            em.wait("vector", "pe", pe_war)
            def f_norm(v):
                return v.tensor_scalar(outbf, xtile, stats[:, 2:3], stats[:, 6:7],
                                       ALU.subtract, ALU.mult)
            return em.op("vector", f_norm, inc="dve")

        def emit_transpose_block(srcbf, dst, col0, stride, n=16):
            last_pe = 0
            for grp in range(0, n, 8):
                cnt = min(8, n - grp)
                em.wait("tensor", "dve", st8.get("tr_evac", 0))
                for j in range(cnt):
                    def f_tr(t, j=j, grp=grp):
                        return t.transpose(psT[:, j * 128:(j + 1) * 128],
                                           srcbf[:, (grp + j) * 128:(grp + j + 1) * 128],
                                           ident[:, :])
                    last_pe = em.op("tensor", f_tr, inc="pe")
                em.wait("vector", "pe", last_pe)
                o_ap = bass.AP(dst, col0 + grp * stride,
                               [[dst.shape[1], 128], [stride, cnt], [1, 128]])
                i_ap = bass.AP(psT, 0, [[1024, 128], [128, cnt], [1, 128]])
                def f_ev(v, o_ap=o_ap, i_ap=i_ap):
                    return v.tensor_copy(o_ap, i_ap)
                st8["tr_evac"] = em.op("vector", f_ev, inc="dve")
            return st8["tr_evac"], last_pe

        def emit_ln1_v():
            def f_wv(g):
                i_ap = bass.AP(wv, 0, [[1024, 128], [128 * 1024, 16], [1, 1024]])
                o_ap = bass.AP(wv_sb, 0, [[16 * 1024, 128], [1024, 16], [1, 1024]])
                return g.dma_start(o_ap, i_ap)
            d_wv = em.op("gpsimd", f_wv, inc="dwv", amt=16)

            ln_done = {}
            trpe = {}
            v_ev = {}

            def emit_pe_tile(tt):
                slot = tt % 3
                nb = xnbuf[:, slot * 2048:(slot + 1) * 2048]
                em.wait("tensor", "dve", ln_done[tt])
                d_tr, p_tr = emit_transpose_block(nb, xn1T, tt * 128, 2048, 16)
                trpe[tt] = p_tr
                if DBG_NO_V:
                    return
                em.wait("tensor", "dwv", d_wv)
                em.wait("tensor", "dve", d_tr)
                for half, bank in ((0, psA), (1, psB)):
                    em.wait("tensor", "dve", v_ev.get((tt - 1, half), 0))
                    p_v = None
                    for dc in range(16):
                        def f_mm(t, tt=tt, half=half, dc=dc, bank=bank):
                            return t.matmul(
                                bank[:, :],
                                xn1T[:, dc * 2048 + tt * 128: dc * 2048 + (tt + 1) * 128],
                                wv_sb[:, dc * 1024 + half * 512: dc * 1024 + (half + 1) * 512],
                                start=(dc == 0), stop=(dc == 15))
                        p_v = em.op("tensor", f_mm, inc=("pe" if dc == 15 else None))
                    em.wait("vector", "pe", p_v)
                    def f_ev(v, tt=tt, half=half, bank=bank):
                        return v.tensor_copy(
                            v_sb[:, tt * 1024 + half * 512: tt * 1024 + (half + 1) * 512],
                            bank[:, :])
                    v_ev[(tt, half)] = em.op("vector", f_ev, inc="dve")

            for tt in range(DBG_LN_TILES):
                slot = tt % 3
                xt = xin[:, slot * 2048:(slot + 1) * 2048]
                em.wait("sync", "dve", ln_done.get(tt - 3, 0))
                def f_x(sy, tt=tt, xt=xt):
                    return sy.dma_start(xt, x_full[tt * 128:(tt + 1) * 128, :])
                d_x = em.op("sync", f_x, inc="dx%d" % slot, amt=16)
                nb = xnbuf[:, slot * 2048:(slot + 1) * 2048]
                d_n = emit_ln(xt, nb, sqx[:, :], wait_dsy=d_x, dsem="dx%d" % slot,
                              pe_war=trpe.get(tt - 3, 0))
                ln_done[tt] = d_n
                if tt > 0:
                    emit_pe_tile(tt - 1)
            emit_pe_tile(DBG_LN_TILES - 1)


        def emit_heads():
            wqkbuf = sb_open("wqkbuf", [128, 2 * 2048], BF16)
            sqh = sb_open("sqh", [128, 512], F32)
            kq2 = sb_open("kq2", [128, 2 * 2048], BF16)
            wqk_ld = {}

            def kq_base(h):
                return (kq_sb, 0) if h % 2 == 0 else (kq2, 0)

            def load_head(h):
                slot = h % 2
                em.wait("gpsimd", "pe", st8.get(("qk_pe", h - 1), 0))
                for ti in range(2):
                    o_ap = bass.AP(wqkbuf, ti * 2048,
                                   [[2 * 2048, 128], [128, 16], [1, 128]])
                    i_ap = bass.AP(wqk, ti * HL * 16 * 128 * 128 + h * 16 * 128 * 128,
                                   [[128, 128], [128 * 128, 16], [1, 128]])
                    def f(g, o_ap=o_ap, i_ap=i_ap):
                        return g.dma_start(o_ap, i_ap)
                    wqk_ld[h] = em.op("gpsimd", f, inc="dwqk%d" % slot, amt=16)

            def production_steps(h):
                if h >= HL:
                    return []
                slot = h % 2
                base = 0
                kqt, _ = kq_base(h)
                steps = []

                def grp(ti, qc, toff):
                    def go():
                        bank = psF
                        em.wait("tensor", "dwqk%d" % slot, wqk_ld[h])
                        em.wait("tensor", "dve", st8.get(("ef_war", bank is psE), 0))
                        p_qk = None
                        for dc in range(16):
                            def f_mm(t, ti=ti, qc=qc, dc=dc, bank=bank):
                                return t.matmul(
                                    bank[:, :],
                                    wqkbuf[:, base + ti * 2048 + dc * 128:
                                           base + ti * 2048 + (dc + 1) * 128],
                                    xn1T[:, dc * 2048 + qc * 512: dc * 2048 + (qc + 1) * 512],
                                    start=(dc == 0), stop=(dc == 15))
                            p_qk = em.op("tensor", f_mm, inc=("pe" if dc == 15 else None))
                        st8[("qk_pe", h)] = p_qk
                        em.wait("vector", "pe", st8.get(("att_done", h - 2), 0))
                        em.wait("vector", "pe", p_qk)
                        def f_ev(v, qc=qc, toff=toff, bank=bank, kqt=kqt):
                            return v.tensor_copy(kqt[:, toff + qc * 512: toff + (qc + 1) * 512],
                                                 bank[:, :])
                        d = em.op("vector", f_ev, inc="dve")
                        st8[("ef_war", bank is psE)] = d
                        st8[("kq_ev", h)] = d
                    return go

                for ti, toff in ((1, 0), (0, 2048)):
                    for qc in range(4):
                        steps.append(grp(ti, qc, toff))

                def norm_chunk(part, cs, mxi):
                    def go():
                        em.wait("scalar", "dve", st8[("kq_ev", h)])
                        em.wait("scalar", "pe", st8.get("nrm_pe", 0))
                        def f_sq(s, part=part, cs=cs, kqt=kqt):
                            return s.activation(
                                sqh[:, :],
                                kqt[:, part * 2048 + cs * 512: part * 2048 + (cs + 1) * 512],
                                AF.Square)
                        a_sq = em.op("scalar", f_sq, inc="act")
                        bank = psF
                        em.wait("tensor", "act", a_sq)
                        em.wait("tensor", "dve", st8.get(("ef_war", bank is psE), 0))
                        def f_mm(t, bank=bank):
                            return t.matmul(bank[:, :], ones128f[:, :], sqh[:, :],
                                            start=True, stop=True)
                        p_n = em.op("tensor", f_mm, inc="pe")
                        st8["nrm_pe"] = p_n
                        em.wait("vector", "pe", p_n)
                        def f_mx(v, mxi=mxi, bank=bank):
                            return v.reduce_max(srow[0:1, 512 + mxi:513 + mxi], bank[0:1, :],
                                                axis=AX.X)
                        d_mx = em.op("vector", f_mx, inc="dve")
                        st8[("ef_war", bank is psE)] = d_mx
                    return go

                mxi = 0
                for part in range(2):
                    for cs in range(4):
                        steps.append(norm_chunk(part, cs, mxi))
                        mxi += 1

                def finale():
                    em.wait("vector", "dve", st8.get(("ef_war", True), 0))
                    em.wait("vector", "dve", st8.get(("ef_war", False), 0))
                    def f_km(v):
                        return v.reduce_max(srow[0:1, 520:521], srow[0:1, 512:516], axis=AX.X)
                    em.op("vector", f_km, inc="dve")
                    def f_qm(v):
                        return v.reduce_max(srow[0:1, 521:522], srow[0:1, 516:520], axis=AX.X)
                    d_qm = em.op("vector", f_qm, inc="dve")
                    em.wait("vector", "dve", d_qm)
                    def f_pr(v):
                        return v.tensor_mul(srow[0:1, 522:523], srow[0:1, 520:521],
                                            srow[0:1, 521:522])
                    d_pr = em.op("vector", f_pr, inc="dve")
                    em.wait("scalar", "dve", d_pr)
                    def f_lnM(s):
                        return s.activation(srow[0:1, 523:524], srow[0:1, 522:523], AF.Ln)
                    a_lnM = em.op("scalar", f_lnM, inc="act")
                    em.wait("scalar", "act", a_lnM)
                    def f_exM(s):
                        return s.activation(srow[0:1, 524:525], srow[0:1, 523:524],
                                            AF.Exp, scale=0.5)
                    a_M = em.op("scalar", f_exM, inc="act")
                    em.wait("tensor", "act", a_M)
                    em.wait("tensor", "dve", st8.get(("ef_war", False), 0))
                    def f_bc(t):
                        return t.matmul(psF[:, 0:1], ones_row[0:1, :], srow[0:1, 524:525],
                                        start=True, stop=True)
                    p_bc = em.op("tensor", f_bc, inc="pe")
                    em.wait("vector", "pe", p_bc)
                    def f_bias(v, slot=slot):
                        return v.tensor_scalar(bias_h[:, slot:slot + 1], psF[:, 0:1],
                                               -INV_SQRT_DH, None, ALU.mult)
                    d_bias = em.op("vector", f_bias, inc="dve")
                    st8[("ef_war", False)] = d_bias
                    st8[("bias", h)] = d_bias
                steps.append(finale)
                return steps

            load_head(0)
            for s in production_steps(0):
                s()
            load_head(1)
            for h in range(HL):
                slot = h % 2
                kqt, _ = kq_base(h)
                pending = production_steps(h + 1)
                if h + 2 < HL:
                    pass
                em.wait("scalar", "dve", st8[("bias", h)])
                em.wait("tensor", "dve", st8[("kq_ev", h)])
                exp_last = st8.setdefault("exp_last", {})
                for qc in range(4):
                    nkc = 4 * qc + 4
                    exp_done = {}
                    u_ready = {}
                    av_done = {}
                    sbanks = (psA, psB, psE)

                    def emit_score(kc):
                        sbank = sbanks[kc % 3]
                        em.wait("tensor", "act",
                                exp_done.get(kc - 3, 0) or exp_last.get(kc % 3, 0))
                        def f_sc(t, kc=kc, sbank=sbank, qc=qc, kqt=kqt):
                            return t.matmul(sbank[:, :],
                                            kqt[:, kc * 128:(kc + 1) * 128],
                                            kqt[:, 2048 + qc * 512: 2048 + (qc + 1) * 512],
                                            start=True, stop=True)
                        p_sc = em.op("tensor", f_sc, inc="pe")
                        em.wait("scalar", "pe", p_sc)
                        em.wait("scalar", "pe", av_done.get(kc - 3, 0))
                        def f_exp(s, kc=kc, sbank=sbank, slot=slot):
                            return s.activation(u_sb[:, (kc % 3) * 512:(kc % 3 + 1) * 512],
                                                sbank[:, :], AF.Exp,
                                                bias=bias_h[:, slot:slot + 1],
                                                scale=INV_SQRT_DH)
                        exp_done[kc] = em.op("scalar", f_exp, inc="act")
                        exp_last[kc % 3] = exp_done[kc]
                        j = kc - 4 * qc
                        if j >= 0:
                            em.wait("vector", "act", exp_done[kc])
                            def f_mask(v, kc=kc, j=j):
                                sl = u_sb[:, (kc % 3) * 512:(kc % 3 + 1) * 512]
                                return v.tensor_mul(sl, sl, masks[:, j * 512:(j + 1) * 512])
                            u_ready[kc] = ("dve", em.op("vector", f_mask, inc="dve"))
                        else:
                            u_ready[kc] = ("act", exp_done[kc])

                    def emit_sumav(kc):
                        eng, val = u_ready[kc]
                        em.wait("tensor", eng, val)
                        if kc == 0:
                            em.wait("tensor", "dve", st8.get("d_inv", 0))
                            em.wait("tensor", "dve", st8.get("z_rd", 0))
                        def f_sum(t, kc=kc, nkc=nkc):
                            return t.matmul(psD[:, :], ones128[:, :],
                                            u_sb[:, (kc % 3) * 512:(kc % 3 + 1) * 512],
                                            start=(kc == 0), stop=(kc == nkc - 1))
                        em.op("tensor", f_sum)
                        def f_av(t, kc=kc, h=h, nkc=nkc):
                            return t.matmul(psC[:, :],
                                            v_sb[:, kc * 1024 + h * 128: kc * 1024 + (h + 1) * 128],
                                            u_sb[:, (kc % 3) * 512:(kc % 3 + 1) * 512],
                                            start=(kc == 0), stop=(kc == nkc - 1))
                        av_done[kc] = em.op("tensor", f_av, inc="pe")

                    for kc in range(nkc):
                        emit_score(kc)
                        if kc >= 2:
                            emit_sumav(kc - 2)
                        if kc % 2 == 0 and pending:
                            pending.pop(0)()
                        elif not DBG_NO_QUANT:
                            quant_pump(1)
                        if kc % 2 == 1 and not pending and not DBG_NO_QUANT:
                            quant_pump(1)
                    emit_sumav(nkc - 2)
                    emit_sumav(nkc - 1)
                    p_att = av_done[nkc - 1]
                    em.wait("vector", "pe", p_att)
                    def f_inv(v):
                        return v.reciprocal(invs_sb[:, :], psD[:, :])
                    st8["d_inv"] = em.op("vector", f_inv, inc="dve")
                    def f_z(v, h=h, qc=qc):
                        return v.tensor_mul(
                            z_sb[:, h * 2048 + qc * 512: h * 2048 + (qc + 1) * 512],
                            psC[:, :], invs_sb[:, :])
                    st8["z_rd"] = em.op("vector", f_z, inc="dve")

                st8[("att_done", h)] = em.cnt["pe"]
                while pending:
                    pending.pop(0)()
                if h + 2 < HL:
                    load_head(h + 2)
            sb_close("kq2")
            sb_close("sqh")
            sb_close("wqkbuf")

        def emit_wo_loads():
            wo_sb = sb_open("wo_sb", [128, HL * 2048], BF16)
            em.wait("gpsimd", "pe", st8[("qk_pe", HL - 1)])
            last_ld = 0
            for fc in range(HL):
                def f(g, fc=fc):
                    return g.dma_start(wo_sb[:, fc * 2048:(fc + 1) * 2048], wo[fc][:, :])
                last_ld = em.op("gpsimd", f, inc="dgp", amt=16)
            return wo_sb, last_ld

        def emit_wo(wo_sb, last_ld):
            wo_stage = sb_open("wo_stage", [128, 2 * 2048], F32)
            stg = {}
            store = {}
            em.wait("tensor", "dgp", last_ld)
            order = [0, 1, 8, 9, 2, 3, 10, 11, 4, 5, 12, 13, 6, 7, 14, 15]
            for idx, tt in enumerate(order):
                slot = idx % 2
                stt_ = wo_stage[:, slot * 2048:(slot + 1) * 2048]
                for dc in range(4):
                    bank = (psA, psB)[dc % 2]
                    em.wait("tensor", "dve",
                            stg.get((idx, dc - 2), 0) or stg.get((idx - 1, dc + 2), 0))
                    p_wo = None
                    for fc in range(HL):
                        def f_mm(t, tt=tt, dc=dc, fc=fc, bank=bank):
                            return t.matmul(
                                bank[:, :],
                                z_sb[:, fc * 2048 + tt * 128: fc * 2048 + (tt + 1) * 128],
                                wo_sb[:, fc * 2048 + dc * 512: fc * 2048 + (dc + 1) * 512],
                                start=(fc == 0), stop=(fc == HL - 1))
                        p_wo = em.op("tensor", f_mm, inc=("pe" if fc == HL - 1 else None))
                    em.wait("vector", "pe", p_wo)
                    em.wait("vector", "dwo%d" % slot, store.get(idx - 2, 0))
                    def f_ev(v, dc=dc, stt_=stt_, bank=bank):
                        return v.tensor_copy(stt_[:, dc * 512:(dc + 1) * 512], bank[:, :])
                    stg[(idx, dc)] = em.op("vector", f_ev, inc="dve")
                em.wait("sync", "dve", stg[(idx, 3)])
                rank, q = tt // 8, tt % 8
                pos = (q // 2) * 512 + rank * 256 + (q % 2) * 128
                def f_st(sy, pos=pos, stt_=stt_):
                    return sy.dma_start(wo_part[pos:pos + 128, :], stt_)
                store[idx] = em.op("sync", f_st, inc="dwo%d" % slot, amt=16)
                if idx % 4 == 3:
                    em.wait("gpsimd", "dwo0", store[idx - 1])
                    em.wait("gpsimd", "dwo1", store[idx])
                    c = idx // 4
                    def f_rs(g, c=c):
                        return g.collective_compute(
                            "ReduceScatter", ALU.add,
                            replica_groups=[[0, 1], [2, 3], [4, 5], [6, 7]],
                            ins=[wo_part[c * 512:(c + 1) * 512, :].opt()],
                            outs=[attn_out[c * 256:(c + 1) * 256, :].opt()])
                    em.op("gpsimd", f_rs, inc="ccrs")
            sb_close("wo_stage")

        def emit_ln2():
            xn2T = sb_open("xn2T", [128, 16 * 1024], BF16, side="right")
            abuf = sb_open("abuf", [128, 2 * 2048], F32)
            rbuf = sb_open("rbuf", [128, 2 * 2048], F32)
            sqx2 = sb_open("sqx2", [128, 2048], F32)
            nb2 = sb_open("nb2", [128, 2 * 2048], BF16)
            ln_done = {}
            trpe = {}
            for tl in range(8):
                em.wait("sync", "ccrs", tl // 2 + 1)
                slot = tl % 2
                at = abuf[:, slot * 2048:(slot + 1) * 2048]
                rt = rbuf[:, slot * 2048:(slot + 1) * 2048]
                em.wait("sync", "dve", ln_done.get(tl - 2, 0))
                def f_a(sy, tl=tl, at=at):
                    return sy.dma_start(at, attn_out[tl * 128:(tl + 1) * 128, :])
                em.op("sync", f_a, inc="dsy", amt=16)
                def f_r(sy, tl=tl, rt=rt):
                    return sy.dma_start(rt, x_res[tl * 128:(tl + 1) * 128, :])
                d_r = em.op("sync", f_r, inc="dsy", amt=16)
                em.wait("vector", "dsy", d_r)
                def f_add(v, at=at, rt=rt):
                    return v.tensor_add(rt, rt, at)
                d_x1 = em.op("vector", f_add, inc="dve")
                em.wait("vector", "dve", d_x1)
                em.wait("sync", "dve", d_x1)
                def f_st(sy, tl=tl, rt=rt):
                    return sy.dma_start(x1d[tl * 128:(tl + 1) * 128, :], rt)
                em.op("sync", f_st, inc="dsy", amt=16)
                nb = nb2[:, slot * 2048:(slot + 1) * 2048]
                d_n = emit_ln(rt, nb, sqx2[:, :], pe_war=trpe.get(tl - 2, 0))
                ln_done[tl] = d_n
                em.wait("tensor", "dve", d_n)
                _, p_tr = emit_transpose_block(nb, xn2T, tl * 128, 1024, 16)
                trpe[tl] = p_tr
            sb_close("nb2")
            sb_close("sqx2")
            sb_close("rbuf")
            sb_close("abuf")
            return xn2T

        def emit_ffn(xn2T):
            h_sb = sb_open("h_sb", [128, 64 * 512], BF16)
            wgu_t = sb_open("wgu_t", [128, 2 * 2 * 2048], BF16)
            sg_buf = sb_open("sg_buf", [128, 2 * 512], F32)
            h2_buf = sb_open("h2_buf", [128, 2 * 512], F32)
            wp_t = sb_open("wp_t", [128, 2 * 8 * 512], BF16)
            x1buf = sb_open("x1buf", [128, 4 * 2048], F32)

            em.wait("sync", "ccgu", em.cnt["ccgu"])
            em.wait("sync", "ccp", em.cnt["ccp"])

            def load_gu(gpc):
                th2, pc = divmod(gpc, 64)
                slot = gpc % 2
                base = slot * 2 * 2048
                em.wait("sync", "pe", st8.get(("gu_pe", gpc - 2), 0))
                last = 0
                for gi, hc in ((0, pc), (1, pc + 64)):
                    o_ap = bass.AP(wgu_t, base + gi * 2048,
                                   [[2 * 2 * 2048, 128], [128, 16], [1, 128]])
                    i_ap = bass.AP(cc_gu_out, hc * 2048 * 128,
                                   [[2048, 128], [128, 16], [1, 128]])
                    def f(sy, o_ap=o_ap, i_ap=i_ap):
                        return sy.dma_start(o_ap, i_ap)
                    last = em.op("sync", f, inc="dgu%d" % slot, amt=16)
                st8[("gu_ld", gpc)] = last

            def load_wp(th2, dc, r, slot):
                em.wait("sync", "pe", st8.get(("wp_war", slot), 0))
                o_ap = bass.AP(wp_t, slot * 8 * 512,
                               [[2 * 8 * 512, 128], [512, 8], [1, 512]])
                i_ap = bass.AP(cc_p_out, (r * 4096 + dc * 1024) * 512,
                               [[512, 128], [128 * 512, 8], [1, 512]])
                def f(sy, o_ap=o_ap, i_ap=i_ap):
                    return sy.dma_start(o_ap, i_ap)
                return em.op("sync", f, inc="dwp%d" % slot, amt=16)

            load_gu(0)
            for th2 in range(2):
                def emit_ssq(pc):
                    em.wait("tensor", "dve", st8[("h2", pc)])
                    if pc == 0:
                        em.wait("tensor", "dve", st8.get("rstd_rd", 0))
                    def f_ssq(t, pc=pc):
                        return t.matmul(psE[:, :], ones128f[:, :],
                                        h2_buf[:, (pc % 2) * 512:(pc % 2 + 1) * 512],
                                        start=(pc == 0), stop=(pc == 63))
                    st8[("ssq_pe", pc)] = em.op("tensor", f_ssq, inc="pe")

                for pc in range(64):
                    gpc = th2 * 64 + pc
                    slot = gpc % 2
                    base = slot * 2 * 2048
                    if pc == 0:
                        for j in range(4):
                            em.wait("tensor", "dve", st8.get(("prj_war", j), 0))
                    if gpc + 1 < 128:
                        load_gu(gpc + 1)
                    em.wait("tensor", "dgu%d" % (gpc % 2), st8[("gu_ld", gpc)])
                    gbank = (psA, psB)[gpc % 2]
                    ubank = (psC, psD)[gpc % 2]
                    em.wait("tensor", "act", st8.get(("silu_war", gbank is psA), 0))
                    em.wait("tensor", "dve", st8.get(("h_war", ubank is psC), 0))
                    p_g = None
                    for dc in range(16):
                        def f_g(t, dc=dc, th2=th2, gbank=gbank, base=base):
                            return t.matmul(
                                gbank[:, :],
                                wgu_t[:, base + dc * 128: base + (dc + 1) * 128],
                                xn2T[:, dc * 1024 + th2 * 512: dc * 1024 + (th2 + 1) * 512],
                                start=(dc == 0), stop=(dc == 15))
                        p_g = em.op("tensor", f_g, inc=("pe" if dc == 15 else None))
                    p_u = None
                    for dc in range(16):
                        def f_u(t, dc=dc, th2=th2, ubank=ubank, base=base):
                            return t.matmul(
                                ubank[:, :],
                                wgu_t[:, base + 2048 + dc * 128: base + 2048 + (dc + 1) * 128],
                                xn2T[:, dc * 1024 + th2 * 512: dc * 1024 + (th2 + 1) * 512],
                                start=(dc == 0), stop=(dc == 15))
                        p_u = em.op("tensor", f_u, inc=("pe" if dc == 15 else None))
                    st8[("gu_pe", gpc)] = p_u
                    if pc > 0:
                        emit_ssq(pc - 1)
                    em.wait("scalar", "pe", p_g)
                    em.wait("scalar", "dve", st8.get(("sg_war", pc % 2), 0))
                    def f_silu(s, gbank=gbank, pc=pc):
                        return s.activation(sg_buf[:, (pc % 2) * 512:(pc % 2 + 1) * 512],
                                            gbank[:, :], AF.Silu)
                    a_silu = em.op("scalar", f_silu, inc="act")
                    st8[("silu_war", gbank is psA)] = a_silu
                    em.wait("vector", "act", a_silu)
                    em.wait("vector", "pe", p_u)
                    em.wait("vector", "pe", st8.get("proj_done", 0))
                    hsl = h_sb[:, pc * 512:(pc + 1) * 512]
                    def f_h(v, hsl=hsl, ubank=ubank, pc=pc):
                        return v.tensor_mul(hsl, sg_buf[:, (pc % 2) * 512:(pc % 2 + 1) * 512],
                                            ubank[:, :])
                    d_h = em.op("vector", f_h, inc="dve")
                    st8[("h_war", ubank is psC)] = d_h
                    st8[("sg_war", pc % 2)] = d_h
                    em.wait("vector", "pe", st8.get(("ssq_pe", pc - 2), 0))
                    def f_h2(v, hsl=hsl, pc=pc):
                        return v.tensor_mul(h2_buf[:, (pc % 2) * 512:(pc % 2 + 1) * 512],
                                            hsl, hsl)
                    st8[("h2", pc)] = em.op("vector", f_h2, inc="dve")
                emit_ssq(63)
                em.wait("scalar", "pe", st8[("ssq_pe", 63)])
                def f_lnS(s):
                    return s.activation(srow[0:1, 0:512], psE[0:1, :], AF.Ln,
                                        bias=epsb[0:1, 1:2], scale=1.0 / HID)
                a_lnS = em.op("scalar", f_lnS, inc="act")
                em.wait("scalar", "act", a_lnS)
                def f_exS(s):
                    return s.activation(srow[0:1, 512:1024], srow[0:1, 0:512],
                                        AF.Exp, scale=-0.5)
                a_r = em.op("scalar", f_exS, inc="act")
                em.wait("tensor", "act", a_r)
                for j in range(4):
                    def f_rc(t, j=j):
                        return t.matmul(psE[:, j:j + 1],
                                        srow[0:1, 512 + j * 128: 512 + (j + 1) * 128],
                                        ones_row[0:1, 0:1], start=True, stop=True)
                    p_rc = em.op("tensor", f_rc, inc=("pe" if j == 3 else None))
                em.wait("vector", "pe", p_rc)
                def f_evR(v):
                    return v.tensor_copy(stats[:, 8:12], psE[:, 0:4])
                d_evR = em.op("vector", f_evR, inc="dve")
                st8["rstd_rd"] = d_evR
                x1_ld = {}
                for j in range(4):
                    tt = th2 * 4 + j
                    em.wait("sync", "dve", st8.get(("x1_rd", j), 0))
                    def f_x1(sy, tt=tt, j=j):
                        return sy.dma_start(x1buf[:, j * 2048:(j + 1) * 2048],
                                            x1d[tt * 128:(tt + 1) * 128, :])
                    x1_ld[j] = em.op("sync", f_x1, inc="dsy", amt=16)
                em.wait("tensor", "dve", st8[("h2", 63)])
                banks4 = (psA, psB, psC, psD)
                for dc in range(4):
                    wp_ld = {0: load_wp(th2, dc, 0, 0)}
                    p_r = 0
                    for r in range(8):
                        slot = r % 2
                        if r + 1 < 8:
                            wp_ld[r + 1] = load_wp(th2, dc, r + 1, (r + 1) % 2)
                        em.wait("tensor", "dwp%d" % (r % 2), wp_ld[r])
                        for rc in range(8):
                            hc = r * 8 + rc
                            for j in range(4):
                                if hc == 0:
                                    em.wait("tensor", "dve",
                                            st8.get(("prj_war", j), 0))
                                def f_mm(t, hc=hc, j=j, rc=rc, slot=slot, th2=th2):
                                    tl = th2 * 4 + j - th2 * 4
                                    return t.matmul(
                                        banks4[j][:, :],
                                        h_sb[:, hc * 512 + j * 128: hc * 512 + (j + 1) * 128],
                                        wp_t[:, slot * 8 * 512 + rc * 512:
                                             slot * 8 * 512 + (rc + 1) * 512],
                                        start=(hc == 0), stop=(hc == 63))
                                inc = "pe" if (rc == 7 and j == 3) else None
                                p_r = em.op("tensor", f_mm, inc=inc)
                        st8[("wp_war", slot)] = p_r
                    for j in range(4):
                        em.wait("vector", "pe", p_r)
                        em.wait("vector", "dsy", x1_ld[3])
                        em.wait("vector", "dve", st8["rstd_rd"])
                        def f_ev(v, j=j, dc=dc):
                            sl = x1buf[:, j * 2048 + dc * 512: j * 2048 + (dc + 1) * 512]
                            return v.scalar_tensor_tensor(sl, banks4[j][:, :],
                                                          stats[:, 8 + j:9 + j], sl,
                                                          ALU.mult, ALU.add)
                        st8[("prj_war", j)] = em.op("vector", f_ev, inc="dve")
                for j in range(4):
                    tt = th2 * 4 + j
                    em.wait("sync", "dve", st8[("prj_war", j)])
                    st8[("x1_rd", j)] = st8[("prj_war", j)]
                    def f_out(sy, tt=tt, j=j):
                        return sy.dma_start(out_t[tt * 128:(tt + 1) * 128, :],
                                            x1buf[:, j * 2048:(j + 1) * 2048])
                    em.op("sync", f_out, inc="dsy", amt=16)
                st8["proj_done"] = st8.get(("wp_war", 0), 0) or 0
                st8["proj_done"] = max(st8.get("proj_done", 0),
                                       st8.get(("wp_war", 1), 0) or 0)

        emit_consts()
        emit_ln1_v()
        sb_close("xnbuf")
        sb_close("wv_sb")
        sb_close("sqx")
        sb_close("xin")
        if stage in ("heads", "wo", "ln2", "full"):
            z_sb = sb_open("z_sb", [128, HL * 2048], BF16, side="right")
            qwbf = sb_open("qwbf", [128, 2 * 2048], BF16, side="right")
            qabs = sb_open("qabs", [128, 2 * 2048], BF16, side="right")
            qsgn = sb_open("qsgn", [128, 2 * 2048], BF16, side="right")
            qg = sb_open("qg", [128, 2 * 2048], BF16, side="right")
            qsbf = sb_open("qsbf", [128, 512], BF16, side="right")
            em.fence()
            emit_heads()
            sb_close("xn1T")
            while qs["done"] < len(q_tiles):
                quant_step()
            quant_gathers()
            wo_sb, wo_last_ld = emit_wo_loads()
        if stage in ("wo", "ln2", "full"):
            em.fence()
            emit_wo(wo_sb, wo_last_ld)
            for nm in ("qsbf", "qg", "qsgn", "qabs", "qwbf"):
                sb_close(nm)
            sb_close("wo_sb")
            sb_close("v_sb")
            sb_close("z_sb")
        xn2T = None
        if stage in ("ln2", "full"):
            em.fence()
            xn2T = emit_ln2()
        if stage == "full":
            em.fence()
            emit_ffn(xn2T)
            for nm in ("x1buf", "wp_t", "h2_buf", "sg_buf", "wgu_t", "h_sb"):
                sb_close(nm)
            sb_close("xn2T")

        tap_outs = []
        for name, key, ncols in taps:
            tensor = reg[key]
            o = nc.dram_tensor("tap_" + name, [128, ncols], tensor.dtype,
                               kind="ExternalOutput")
            tap_outs.append((o, tensor, ncols))
        if tap_outs:
            em.fence(("sync",), exclude=())
            for o, tensor, ncols in tap_outs:
                def f(sy, o=o, tensor=tensor, ncols=ncols):
                    return sy.dma_start(o[:, :], tensor[:, 0:ncols])
                em.op("sync", f, inc="dsy", amt=16)

        def f_fin(sy):
            return sy.wait_ge(sems["dsy"], em.cnt["dsy"])
        em.q["sync"].append(f_fin)
        def f_fin2(g):
            return g.wait_ge(sems["dgp"], em.cnt["dgp"])
        em.q["gpsimd"].append(f_fin2)

        with nc.Block() as block:
            @block.tensor
            def _(t):
                for f in em.q["tensor"]:
                    f(t)

            @block.vector
            def _(v):
                for f in em.q["vector"]:
                    f(v)

            @block.scalar
            def _(s):
                for f in em.q["scalar"]:
                    f(s)

            @block.gpsimd
            def _(g):
                for f in em.q["gpsimd"]:
                    f(g)

            @block.sync
            def _(sy):
                for f in em.q["sync"]:
                    f(sy)

        for name in list(reversed(list(_open_cms))):
            sb_close(name)

    return nc


def make_inputs(L0_attn_x, L0_attn_W_K, L0_attn_W_Q, L0_attn_W_V, L0_attn_W_O,
                L0_ffn_w_gate_up, L0_ffn_w_proj):
    x = np.asarray(L0_attn_x, np.float32)
    W_K = np.asarray(L0_attn_W_K, np.float32)
    W_Q = np.asarray(L0_attn_W_Q, np.float32)
    W_V = np.asarray(L0_attn_W_V, np.float32)
    W_O = np.asarray(L0_attn_W_O, np.float32)
    wgu_f = np.asarray(L0_ffn_w_gate_up, np.float32)
    wp_f = np.asarray(L0_ffn_w_proj, np.float32)
    woT = np.ascontiguousarray(W_O.T)
    in_maps = []
    for c in range(NCORES):
        b, par = c // 2, c % 2
        hr = slice(par * HL, (par + 1) * HL)
        wq_t = W_Q[hr].transpose(0, 2, 1).reshape(HL, 16, 128, 128)
        wk_t = W_K[hr].transpose(0, 2, 1).reshape(HL, 16, 128, 128)
        wqk_h = np.ascontiguousarray(np.stack([wq_t, wk_t]))
        wv_t = np.ascontiguousarray(
            W_V[hr].transpose(0, 2, 1).reshape(HL, 16, 128, 128)
            .transpose(1, 2, 0, 3)).reshape(16, 128, HL * 128)
        wo_my = np.ascontiguousarray(woT[par * 1024:(par + 1) * 1024]).reshape(HL, 128, D)
        wgu_sh = np.ascontiguousarray(wgu_f[c * 2048:(c + 1) * 2048, :].T).reshape(16, 128, 2048)
        wp_sh = np.ascontiguousarray(wp_f[:, c * 1024:(c + 1) * 1024].T).reshape(8, 128, 2048)
        in_maps.append({
            "x_full": np.ascontiguousarray(x[b]),
            "x_res": np.ascontiguousarray(x[b, par * TOK:(par + 1) * TOK]),
            "wqk": wqk_h,
            "wv": wv_t,
            "wo": wo_my,
            "wgu": wgu_sh,
            "wp": wp_sh,
        })
    return in_maps


def kernel(L0_attn_x, L0_attn_W_K, L0_attn_W_Q, L0_attn_W_V, L0_attn_W_O,
           L0_ffn_w_gate_up, L0_ffn_w_proj):
    in_maps = make_inputs(L0_attn_x, L0_attn_W_K, L0_attn_W_Q, L0_attn_W_V,
                          L0_attn_W_O, L0_ffn_w_gate_up, L0_ffn_w_proj)
    nc = build("full")
    res = run_bass_kernel_spmd(nc, in_maps, core_ids=list(range(NCORES)))
    out = np.empty((B, S, D), np.float32)
    for c in range(NCORES):
        b, par = c // 2, c % 2
        out[b, par * TOK:(par + 1) * TOK] = res.results[c]["out"]
    return out


def build_with_taps(stage, taps, **kw):
    return build(stage, taps)


# version 14
# speedup vs baseline: 1.2855x; 1.0034x over previous
import sys

sys.path.insert(0, "/opt/trn_rl_repo")

import numpy as np
import concourse.bass as bass
import concourse.mybir as mybir
from concourse.bass_utils import run_bass_kernel_spmd

F32 = mybir.dt.float32
BF16 = mybir.dt.bfloat16
AF = mybir.ActivationFunctionType
ALU = mybir.AluOpType
AX = mybir.AxisListType

NCORES = 8
B, S, D = 4, 2048, 2048
H, DH = 16, 128
HL = H // 2
TOK = S // 2
HID = 4 * D
INV_SQRT_DH = float(1.0 / np.sqrt(np.float32(DH)))
LN_EPS = 1e-5
RMS_EPS = float(np.finfo(np.float32).eps)
THETA = 0.5 + 2.0 ** -10

ENGS = ("tensor", "vector", "scalar", "gpsimd", "sync")


class Em:

    def __init__(self, sems):
        self.sems = sems
        self.q = {e: [] for e in ENGS}
        self.cnt = {k: 0 for k in sems}
        self._lastwait = {}

    def op(self, eng, fn, inc=None, amt=1):
        if inc is None:
            self.q[eng].append(fn)
            return None
        self.cnt[inc] += amt
        sem = self.sems[inc]
        self.q[eng].append(lambda E, f=fn, s=sem, a=amt: f(E).then_inc(s, a))
        return self.cnt[inc]

    def wait(self, eng, sem_name, val):
        if not val:
            return
        key = (eng, sem_name)
        if self._lastwait.get(key, 0) >= val:
            return
        assert val <= self.cnt[sem_name], \
            f"wait {sem_name}>={val} but only {self.cnt[sem_name]} emitted"
        self._lastwait[key] = val
        s = self.sems[sem_name]
        self.q[eng].append(lambda E, s=s, v=val: E.wait_ge(s, v))

    def fence(self, engs=ENGS, exclude=("ccgu", "ccp", "ccrs", "dgp", "dq0", "dq1",
                                         "dwqk0", "dwqk1", "dgu0", "dgu1",
                                         "dwp0", "dwp1", "dwv", "dx0", "dx1")):
        snap = dict(self.cnt)
        for e in engs:
            for k, v in snap.items():
                if v and k not in exclude:
                    self.wait(e, k, v)


DBG_NO_V = False
DBG_NO_QUANT = False
DBG_LN_TILES = 16


def build(stage="full", taps=()):
    nc = bass.Bass("TRN2", target_bir_lowering=False)

    x_full = nc.dram_tensor("x_full", [S, D], F32, kind="ExternalInput")
    x_res = nc.dram_tensor("x_res", [TOK, D], F32, kind="ExternalInput")
    wqk = nc.dram_tensor("wqk", [2, HL, 16, 128, 128], F32, kind="ExternalInput")
    wv = nc.dram_tensor("wv", [16, 128, HL * 128], F32, kind="ExternalInput")
    wo = nc.dram_tensor("wo", [HL, 128, D], F32, kind="ExternalInput")
    wgu = nc.dram_tensor("wgu", [16, 128, 2048], F32, kind="ExternalInput")
    wp = nc.dram_tensor("wp", [8, 128, 2048], F32, kind="ExternalInput")
    out_t = nc.dram_tensor("out", [TOK, D], F32, kind="ExternalOutput")

    cc_gu_in = nc.dram_tensor("cc_gu_in", [2048 * 16, 128], BF16)
    cc_gu_out = nc.dram_tensor("cc_gu_out", [NCORES * 2048 * 16, 128], BF16,
                               addr_space="Shared")
    cc_p_in = nc.dram_tensor("cc_p_in", [4096, 512], BF16)
    cc_p_out = nc.dram_tensor("cc_p_out", [NCORES * 4096, 512], BF16,
                              addr_space="Shared")
    wo_part = nc.dram_tensor("wo_part", [S, D], F32)
    attn_out = nc.dram_tensor("attn_out", [TOK, D], F32)
    x1d = nc.dram_tensor("x1d", [TOK, D], F32)

    from contextlib import ExitStack
    stack = ExitStack()
    
    ps = lambda name, shape, dt: stack.enter_context(nc.psum_tensor(name, shape, dt))

    _open_cms = {}
    reg = {}

    def sb_open(name, shape, dt, side=None):
        cm = nc.sbuf_tensor(name, shape, dt, side=side) if side else nc.sbuf_tensor(name, shape, dt)
        h = cm.__enter__()
        _open_cms[name] = cm
        reg[name] = h
        return h

    def sb_close(name):
        _open_cms.pop(name).__exit__(None, None, None)

    def sbr(name, shape, dt):
        h = stack.enter_context(nc.sbuf_tensor(name, shape, dt))
        reg[name] = h
        return h

    with stack:
        ident = sbr("ident", [128, 128], BF16)
        masks = sbr("masks", [128, 4 * 512], BF16)
        bones = sbr("bones", [128, 128], BF16)
        ones_col = sbr("ones_col", [128, 1], BF16)
        ones128 = sbr("ones128", [128, 128], BF16)
        ones128f = sbr("ones128f", [128, 128], F32)
        ones_col_f = sbr("ones_col_f", [128, 1], F32)
        epsb = sbr("epsb", [128, 2], F32)
        ones_row = sbr("ones_row", [1, 128], F32)
        stats = sbr("stats", [128, 16], F32)
        srow = sbr("srow", [128, 1024], F32)
        bias_h = sbr("bias_h", [128, 2], F32)
        u_sb = sbr("u_sb", [128, 3 * 512], BF16)
        kq_sb = sbr("kq_sb", [128, 2 * 2048], BF16)
        invs_sb = sbr("invs_sb", [128, 512], F32)

        psA = ps("psA", [128, 512], F32)
        psB = ps("psB", [128, 512], F32)
        psC = ps("psC", [128, 512], F32)
        psD = ps("psD", [128, 512], F32)
        psE = ps("psE", [128, 512], F32)
        psF = ps("psF", [128, 512], F32)
        psG = ps("psG", [128, 512], F32)
        psT = ps("psT", [128, 1024], BF16)

        qwbf = qabs = qsgn = qg = qsbf = None

        v_sb = sb_open("v_sb", [128, 16 * 1024], BF16)
        xn1T = sb_open("xn1T", [128, 16 * 2048], BF16)
        z_sb = None
        xin = sb_open("xin", [128, 2 * 2048], F32)
        sqx = sb_open("sqx", [128, 2048], F32)
        wv_sb = sb_open("wv_sb", [128, 16 * 1024], BF16)
        xnbuf = sb_open("xnbuf", [128, 2 * 2048], BF16)

        sems = {}
        for name in ("pe", "dve", "act", "pool", "dsy", "dgp", "ccgu", "ccp", "ccrs",
                     "dx0", "dx1", "dwv", "dwqk0", "dwqk1", "dq0", "dq1",
                     "dgu0", "dgu1", "dwp0", "dwp1", "dwo0", "dwo1"):
            sems[name] = stack.enter_context(nc.semaphore(name))
        em = Em(sems)
        st8 = {}

        def emit_consts():
            def f(g):
                g.memset(ident[:, :], 1.0)
                g.affine_select(ident[:, :], ident[:, :], [[1, 128]], ALU.is_equal,
                                0.0, base=0, channel_multiplier=-1)
                g.memset(masks[:, :], 1.0)
                for j in range(4):
                    g.affine_select(masks[:, j * 512:(j + 1) * 512],
                                    masks[:, j * 512:(j + 1) * 512],
                                    [[1, 512]], ALU.is_ge, 0.0,
                                    base=-128 * j, channel_multiplier=-1)
                g.memset(bones[:, :], 0.0)
                g.memset(bones[0:64, 0:64], 1.0)
                g.memset(bones[64:128, 64:128], 1.0)
                g.memset(ones_col[:, :], 1.0)
                g.memset(ones128[:, :], 1.0)
                g.memset(ones128f[:, :], 1.0)
                g.memset(ones_col_f[:, :], 1.0)
                g.memset(epsb[:, 0:1], LN_EPS)
                g.memset(epsb[:, 1:2], RMS_EPS)
                return g.memset(ones_row[:, :], 1.0)
            c = em.op("gpsimd", f, inc="pool")
            for e in ("tensor", "vector", "scalar"):
                em.wait(e, "pool", c)

        q_tiles = [("gu", i) for i in range(16)] + [("p", i) for i in range(8)]
        qs = {"done": 0, "issued": 0}
        qpend = []

        def quant_substeps(i):
            kind, idx = q_tiles[i]
            slot = i % 2
            src_t = wgu[idx] if kind == "gu" else wp[idx]
            wb = qwbf[:, slot * 2048:(slot + 1) * 2048]
            ab = qabs[:, slot * 2048:(slot + 1) * 2048]
            sg = qsgn[:, slot * 2048:(slot + 1) * 2048]
            gg = qg[:, slot * 2048:(slot + 1) * 2048]
            steps = []

            def s_load():
                em.wait("gpsimd", "dve", qs.get(("qdve", i - 2, 3), 0))
                def f_ld(g):
                    return g.dma_start(wb, src_t[:, :])
                qs[("ld", i)] = em.op("gpsimd", f_ld, inc="dq%d" % slot, amt=16)
            steps.append(s_load)

            def s_abs():
                em.wait("scalar", "dq%d" % slot, qs[("ld", i)])
                em.wait("scalar", "dve", qs.get(("qdve", i - 2, 3), 0))
                def f_abs(s):
                    return s.activation(ab, wb, AF.Abs)
                qs[("abs", i)] = em.op("scalar", f_abs, inc="act")
                def f_sgn(s):
                    return s.activation(sg, wb, AF.Sign)
                qs[("sgn", i)] = em.op("scalar", f_sgn, inc="act")
            steps.append(s_abs)

            def s_chunk(cs):
                def go():
                    if cs == 0:
                        em.wait("vector", "dgp", qs.get(("out", i - 2), 0))
                    em.wait("tensor", "act", qs[("abs", i)])
                    em.wait("tensor", "dve", qs.get(("sbf", i, cs - 1), 0) or
                            qs.get(("sbf", i - 1, 3), 0))
                    def f_mm(t, cs=cs):
                        return t.matmul(psG[:, :], bones[:, :],
                                        ab[:, cs * 512:(cs + 1) * 512],
                                        start=True, stop=True)
                    p_mm = em.op("tensor", f_mm, inc="pe")
                    em.wait("vector", "pe", p_mm)
                    def f_sbf(v):
                        return v.tensor_scalar(qsbf[:, :], psG[:, :], 1.0 / 64.0, 1e-8,
                                               ALU.mult, ALU.max)
                    d_sbf = em.op("vector", f_sbf, inc="dve")
                    qs[("sbf", i, cs)] = d_sbf
                    em.wait("vector", "dve", d_sbf)
                    em.wait("vector", "act", qs[("sgn", i)])
                    def f_g(v, cs=cs):
                        return v.scalar_tensor_tensor(gg[:, cs * 512:(cs + 1) * 512],
                                                      qsbf[:, :], THETA,
                                                      ab[:, cs * 512:(cs + 1) * 512],
                                                      ALU.mult, ALU.is_lt)
                    em.op("vector", f_g, inc="dve")
                    def f_m(v, cs=cs):
                        sl = gg[:, cs * 512:(cs + 1) * 512]
                        return v.tensor_mul(sl, sl, sg[:, cs * 512:(cs + 1) * 512])
                    qs[("m", i)] = em.op("vector", f_m, inc="dve")
                    def f_wh(v, cs=cs):
                        sl = gg[:, cs * 512:(cs + 1) * 512]
                        return v.tensor_mul(sl, sl, qsbf[:, :])
                    qs[("qdve", i, cs)] = em.op("vector", f_wh, inc="dve")
                return go
            for cs in range(4):
                steps.append(s_chunk(cs))

            def s_out():
                em.wait("gpsimd", "dve", qs[("qdve", i, 3)])
                if kind == "gu":
                    o_ap = bass.AP(cc_gu_in, idx * 128,
                                   [[2048, 128], [262144, 16], [1, 128]])
                    i_ap = bass.AP(qg, slot * 2048, [[2 * 2048, 128], [128, 16], [1, 128]])
                else:
                    o_ap = bass.AP(cc_p_in, idx * 128 * 512,
                                   [[512, 128], [8 * 128 * 512, 4], [1, 512]])
                    i_ap = bass.AP(qg, slot * 2048, [[2 * 2048, 128], [512, 4], [1, 512]])
                def f_out(g):
                    return g.dma_start(o_ap, i_ap)
                qs[("out", i)] = em.op("gpsimd", f_out, inc="dgp", amt=16)
                qs["done"] = i + 1
            steps.append(s_out)
            return steps

        def quant_pump(n):
            for _ in range(n):
                if not qpend and qs["issued"] < len(q_tiles):
                    qpend.extend(quant_substeps(qs["issued"]))
                    qs["issued"] += 1
                if qpend:
                    qpend.pop(0)()

        def quant_step():
            if DBG_NO_QUANT:
                return
            if qs["issued"] < len(q_tiles) or qpend:
                quant_pump(9)

        def quant_gathers():
            em.wait("gpsimd", "dgp", qs[("out", len(q_tiles) - 1)])
            def f_gu(g):
                return g.collective_compute(
                    "AllGather", ALU.bypass, replica_groups=[list(range(NCORES))],
                    ins=[cc_gu_in.ap().opt()], outs=[cc_gu_out.ap().opt()])
            em.op("gpsimd", f_gu, inc="ccgu")
            def f_p(g):
                return g.collective_compute(
                    "AllGather", ALU.bypass, replica_groups=[list(range(NCORES))],
                    ins=[cc_p_in.ap().opt()], outs=[cc_p_out.ap().opt()])
            em.op("gpsimd", f_p, inc="ccp")

        def emit_ln(xtile, outbf, sqbuf, wait_dsy=0, dsem="dsy", pe_war=0):
            em.wait("vector", dsem, wait_dsy)
            def f_s1(v):
                return v.reduce_sum(stats[:, 0:1], xtile, axis=AX.X)
            em.op("vector", f_s1, inc="dve")
            def f_sq(v):
                return v.tensor_mul(sqbuf, xtile, xtile)
            d_sq = em.op("vector", f_sq, inc="dve")
            em.wait("vector", "dve", d_sq)
            def f_s2(v):
                return v.reduce_sum(stats[:, 1:2], sqbuf, axis=AX.X)
            d_s2 = em.op("vector", f_s2, inc="dve")
            st8["sqx_rd"] = d_s2
            def f_mean(v):
                return v.tensor_scalar(stats[:, 2:3], stats[:, 0:1], 1.0 / D, None, ALU.mult)
            d_mean = em.op("vector", f_mean, inc="dve")
            em.wait("vector", "dve", d_mean)
            def f_m2(v):
                return v.tensor_mul(stats[:, 3:4], stats[:, 2:3], stats[:, 2:3])
            d_m2 = em.op("vector", f_m2, inc="dve")
            em.wait("vector", "dve", d_m2)
            def f_var(v):
                return v.scalar_tensor_tensor(stats[:, 4:5], stats[:, 1:2], 1.0 / D,
                                              stats[:, 3:4], ALU.mult, ALU.subtract)
            d_var = em.op("vector", f_var, inc="dve")
            em.wait("scalar", "dve", d_var)
            def f_ln(s):
                return s.activation(stats[:, 5:6], stats[:, 4:5], AF.Ln, bias=epsb[:, 0:1])
            a_ln = em.op("scalar", f_ln, inc="act")
            em.wait("scalar", "act", a_ln)
            def f_ex(s):
                return s.activation(stats[:, 6:7], stats[:, 5:6], AF.Exp, scale=-0.5)
            a_r = em.op("scalar", f_ex, inc="act")
            em.wait("vector", "act", a_r)
            em.wait("vector", "pe", pe_war)
            def f_norm(v):
                return v.tensor_scalar(outbf, xtile, stats[:, 2:3], stats[:, 6:7],
                                       ALU.subtract, ALU.mult)
            return em.op("vector", f_norm, inc="dve")

        def emit_transpose_block(srcbf, dst, col0, stride, n=16):
            last_pe = 0
            for grp in range(0, n, 8):
                cnt = min(8, n - grp)
                em.wait("tensor", "dve", st8.get("tr_evac", 0))
                for j in range(cnt):
                    def f_tr(t, j=j, grp=grp):
                        return t.transpose(psT[:, j * 128:(j + 1) * 128],
                                           srcbf[:, (grp + j) * 128:(grp + j + 1) * 128],
                                           ident[:, :])
                    last_pe = em.op("tensor", f_tr, inc="pe")
                em.wait("vector", "pe", last_pe)
                o_ap = bass.AP(dst, col0 + grp * stride,
                               [[dst.shape[1], 128], [stride, cnt], [1, 128]])
                i_ap = bass.AP(psT, 0, [[1024, 128], [128, cnt], [1, 128]])
                def f_ev(v, o_ap=o_ap, i_ap=i_ap):
                    return v.tensor_copy(o_ap, i_ap)
                st8["tr_evac"] = em.op("vector", f_ev, inc="dve")
            return st8["tr_evac"], last_pe

        def emit_ln1_v():
            def f_wv(g):
                i_ap = bass.AP(wv, 0, [[1024, 128], [128 * 1024, 16], [1, 1024]])
                o_ap = bass.AP(wv_sb, 0, [[16 * 1024, 128], [1024, 16], [1, 1024]])
                return g.dma_start(o_ap, i_ap)
            d_wv = em.op("gpsimd", f_wv, inc="dwv", amt=16)

            ln_done = {}
            trpe = {}
            v_ev = {}

            def emit_pe_tile(tt):
                slot = tt % 2
                nb = xnbuf[:, slot * 2048:(slot + 1) * 2048]
                em.wait("tensor", "dve", ln_done[tt])
                d_tr, p_tr = emit_transpose_block(nb, xn1T, tt * 128, 2048, 16)
                trpe[tt] = p_tr
                if DBG_NO_V:
                    return
                em.wait("tensor", "dwv", d_wv)
                em.wait("tensor", "dve", d_tr)
                for half, bank in ((0, psA), (1, psB)):
                    em.wait("tensor", "dve", v_ev.get((tt - 1, half), 0))
                    p_v = None
                    for dc in range(16):
                        def f_mm(t, tt=tt, half=half, dc=dc, bank=bank):
                            return t.matmul(
                                bank[:, :],
                                xn1T[:, dc * 2048 + tt * 128: dc * 2048 + (tt + 1) * 128],
                                wv_sb[:, dc * 1024 + half * 512: dc * 1024 + (half + 1) * 512],
                                start=(dc == 0), stop=(dc == 15))
                        p_v = em.op("tensor", f_mm, inc=("pe" if dc == 15 else None))
                    em.wait("vector", "pe", p_v)
                    def f_ev(v, tt=tt, half=half, bank=bank):
                        return v.tensor_copy(
                            v_sb[:, tt * 1024 + half * 512: tt * 1024 + (half + 1) * 512],
                            bank[:, :])
                    v_ev[(tt, half)] = em.op("vector", f_ev, inc="dve")

            for tt in range(DBG_LN_TILES):
                slot = tt % 2
                xt = xin[:, slot * 2048:(slot + 1) * 2048]
                em.wait("sync", "dve", ln_done.get(tt - 2, 0))
                def f_x(sy, tt=tt, xt=xt):
                    return sy.dma_start(xt, x_full[tt * 128:(tt + 1) * 128, :])
                d_x = em.op("sync", f_x, inc="dx%d" % slot, amt=16)
                nb = xnbuf[:, slot * 2048:(slot + 1) * 2048]
                d_n = emit_ln(xt, nb, sqx[:, :], wait_dsy=d_x, dsem="dx%d" % slot,
                              pe_war=trpe.get(tt - 2, 0))
                ln_done[tt] = d_n
                if tt > 0:
                    emit_pe_tile(tt - 1)
            emit_pe_tile(DBG_LN_TILES - 1)


        def emit_heads():
            wqkbuf = sb_open("wqkbuf", [128, 2 * 2048], BF16)
            sqh = sb_open("sqh", [128, 512], F32)
            kq2 = sb_open("kq2", [128, 2 * 2048], BF16)
            wqk_ld = {}

            def kq_base(h):
                return (kq_sb, 0) if h % 2 == 0 else (kq2, 0)

            def load_head(h):
                slot = h % 2
                em.wait("gpsimd", "pe", st8.get(("qk_pe", h - 1), 0))
                for ti in range(2):
                    o_ap = bass.AP(wqkbuf, ti * 2048,
                                   [[2 * 2048, 128], [128, 16], [1, 128]])
                    i_ap = bass.AP(wqk, ti * HL * 16 * 128 * 128 + h * 16 * 128 * 128,
                                   [[128, 128], [128 * 128, 16], [1, 128]])
                    def f(g, o_ap=o_ap, i_ap=i_ap):
                        return g.dma_start(o_ap, i_ap)
                    wqk_ld[h] = em.op("gpsimd", f, inc="dwqk%d" % slot, amt=16)

            def production_steps(h):
                if h >= HL:
                    return []
                slot = h % 2
                base = 0
                kqt, _ = kq_base(h)
                steps = []

                def grp(ti, qc, toff):
                    def go():
                        bank = psF
                        em.wait("tensor", "dwqk%d" % slot, wqk_ld[h])
                        em.wait("tensor", "dve", st8.get(("ef_war", bank is psE), 0))
                        p_qk = None
                        for dc in range(16):
                            def f_mm(t, ti=ti, qc=qc, dc=dc, bank=bank):
                                return t.matmul(
                                    bank[:, :],
                                    wqkbuf[:, base + ti * 2048 + dc * 128:
                                           base + ti * 2048 + (dc + 1) * 128],
                                    xn1T[:, dc * 2048 + qc * 512: dc * 2048 + (qc + 1) * 512],
                                    start=(dc == 0), stop=(dc == 15))
                            p_qk = em.op("tensor", f_mm, inc=("pe" if dc == 15 else None))
                        st8[("qk_pe", h)] = p_qk
                        em.wait("vector", "pe", st8.get(("att_done", h - 2), 0))
                        em.wait("vector", "pe", p_qk)
                        def f_ev(v, qc=qc, toff=toff, bank=bank, kqt=kqt):
                            return v.tensor_copy(kqt[:, toff + qc * 512: toff + (qc + 1) * 512],
                                                 bank[:, :])
                        d = em.op("vector", f_ev, inc="dve")
                        st8[("ef_war", bank is psE)] = d
                        st8[("kq_ev", h)] = d
                    return go

                for ti, toff in ((1, 0), (0, 2048)):
                    for qc in range(4):
                        steps.append(grp(ti, qc, toff))

                def norm_chunk(part, cs, mxi):
                    def go():
                        em.wait("scalar", "dve", st8[("kq_ev", h)])
                        em.wait("scalar", "pe", st8.get("nrm_pe", 0))
                        def f_sq(s, part=part, cs=cs, kqt=kqt):
                            return s.activation(
                                sqh[:, :],
                                kqt[:, part * 2048 + cs * 512: part * 2048 + (cs + 1) * 512],
                                AF.Square)
                        a_sq = em.op("scalar", f_sq, inc="act")
                        bank = psF
                        em.wait("tensor", "act", a_sq)
                        em.wait("tensor", "dve", st8.get(("ef_war", bank is psE), 0))
                        def f_mm(t, bank=bank):
                            return t.matmul(bank[:, :], ones128f[:, :], sqh[:, :],
                                            start=True, stop=True)
                        p_n = em.op("tensor", f_mm, inc="pe")
                        st8["nrm_pe"] = p_n
                        em.wait("vector", "pe", p_n)
                        def f_mx(v, mxi=mxi, bank=bank):
                            return v.reduce_max(srow[0:1, 512 + mxi:513 + mxi], bank[0:1, :],
                                                axis=AX.X)
                        d_mx = em.op("vector", f_mx, inc="dve")
                        st8[("ef_war", bank is psE)] = d_mx
                    return go

                mxi = 0
                for part in range(2):
                    for cs in range(4):
                        steps.append(norm_chunk(part, cs, mxi))
                        mxi += 1

                def finale():
                    em.wait("vector", "dve", st8.get(("ef_war", True), 0))
                    em.wait("vector", "dve", st8.get(("ef_war", False), 0))
                    def f_km(v):
                        return v.reduce_max(srow[0:1, 520:521], srow[0:1, 512:516], axis=AX.X)
                    em.op("vector", f_km, inc="dve")
                    def f_qm(v):
                        return v.reduce_max(srow[0:1, 521:522], srow[0:1, 516:520], axis=AX.X)
                    d_qm = em.op("vector", f_qm, inc="dve")
                    em.wait("vector", "dve", d_qm)
                    def f_pr(v):
                        return v.tensor_mul(srow[0:1, 522:523], srow[0:1, 520:521],
                                            srow[0:1, 521:522])
                    d_pr = em.op("vector", f_pr, inc="dve")
                    em.wait("scalar", "dve", d_pr)
                    def f_lnM(s):
                        return s.activation(srow[0:1, 523:524], srow[0:1, 522:523], AF.Ln)
                    a_lnM = em.op("scalar", f_lnM, inc="act")
                    em.wait("scalar", "act", a_lnM)
                    def f_exM(s):
                        return s.activation(srow[0:1, 524:525], srow[0:1, 523:524],
                                            AF.Exp, scale=0.5)
                    a_M = em.op("scalar", f_exM, inc="act")
                    em.wait("tensor", "act", a_M)
                    em.wait("tensor", "dve", st8.get(("ef_war", False), 0))
                    def f_bc(t):
                        return t.matmul(psF[:, 0:1], ones_row[0:1, :], srow[0:1, 524:525],
                                        start=True, stop=True)
                    p_bc = em.op("tensor", f_bc, inc="pe")
                    em.wait("vector", "pe", p_bc)
                    def f_bias(v, slot=slot):
                        return v.tensor_scalar(bias_h[:, slot:slot + 1], psF[:, 0:1],
                                               -INV_SQRT_DH, None, ALU.mult)
                    d_bias = em.op("vector", f_bias, inc="dve")
                    st8[("ef_war", False)] = d_bias
                    st8[("bias", h)] = d_bias
                steps.append(finale)
                return steps

            load_head(0)
            for s in production_steps(0):
                s()
            load_head(1)
            for h in range(HL):
                slot = h % 2
                kqt, _ = kq_base(h)
                pending = production_steps(h + 1)
                if h + 2 < HL:
                    pass
                em.wait("scalar", "dve", st8[("bias", h)])
                em.wait("tensor", "dve", st8[("kq_ev", h)])
                exp_last = st8.setdefault("exp_last", {})
                for qc in range(4):
                    nkc = 4 * qc + 4
                    exp_done = {}
                    u_ready = {}
                    av_done = {}
                    sbanks = (psA, psB, psE)

                    def emit_score(kc):
                        sbank = sbanks[kc % 3]
                        em.wait("tensor", "act",
                                exp_done.get(kc - 3, 0) or exp_last.get(kc % 3, 0))
                        def f_sc(t, kc=kc, sbank=sbank, qc=qc, kqt=kqt):
                            return t.matmul(sbank[:, :],
                                            kqt[:, kc * 128:(kc + 1) * 128],
                                            kqt[:, 2048 + qc * 512: 2048 + (qc + 1) * 512],
                                            start=True, stop=True)
                        p_sc = em.op("tensor", f_sc, inc="pe")
                        em.wait("scalar", "pe", p_sc)
                        em.wait("scalar", "pe", av_done.get(kc - 3, 0))
                        def f_exp(s, kc=kc, sbank=sbank, slot=slot):
                            return s.activation(u_sb[:, (kc % 3) * 512:(kc % 3 + 1) * 512],
                                                sbank[:, :], AF.Exp,
                                                bias=bias_h[:, slot:slot + 1],
                                                scale=INV_SQRT_DH)
                        exp_done[kc] = em.op("scalar", f_exp, inc="act")
                        exp_last[kc % 3] = exp_done[kc]
                        j = kc - 4 * qc
                        if j >= 0:
                            em.wait("vector", "act", exp_done[kc])
                            def f_mask(v, kc=kc, j=j):
                                sl = u_sb[:, (kc % 3) * 512:(kc % 3 + 1) * 512]
                                return v.tensor_mul(sl, sl, masks[:, j * 512:(j + 1) * 512])
                            u_ready[kc] = ("dve", em.op("vector", f_mask, inc="dve"))
                        else:
                            u_ready[kc] = ("act", exp_done[kc])

                    def emit_sumav(kc):
                        eng, val = u_ready[kc]
                        em.wait("tensor", eng, val)
                        if kc == 0:
                            em.wait("tensor", "dve", st8.get("d_inv", 0))
                            em.wait("tensor", "dve", st8.get("z_rd", 0))
                        def f_sum(t, kc=kc, nkc=nkc):
                            return t.matmul(psD[:, :], ones128[:, :],
                                            u_sb[:, (kc % 3) * 512:(kc % 3 + 1) * 512],
                                            start=(kc == 0), stop=(kc == nkc - 1))
                        em.op("tensor", f_sum)
                        def f_av(t, kc=kc, h=h, nkc=nkc):
                            return t.matmul(psC[:, :],
                                            v_sb[:, kc * 1024 + h * 128: kc * 1024 + (h + 1) * 128],
                                            u_sb[:, (kc % 3) * 512:(kc % 3 + 1) * 512],
                                            start=(kc == 0), stop=(kc == nkc - 1))
                        av_done[kc] = em.op("tensor", f_av, inc="pe")

                    for kc in range(nkc):
                        emit_score(kc)
                        if kc >= 2:
                            emit_sumav(kc - 2)
                        if kc % 2 == 0 and pending:
                            pending.pop(0)()
                        elif not DBG_NO_QUANT:
                            quant_pump(1)
                        if kc % 2 == 1 and not pending and not DBG_NO_QUANT:
                            quant_pump(1)
                    emit_sumav(nkc - 2)
                    emit_sumav(nkc - 1)
                    p_att = av_done[nkc - 1]
                    em.wait("vector", "pe", p_att)
                    def f_inv(v):
                        return v.reciprocal(invs_sb[:, :], psD[:, :])
                    st8["d_inv"] = em.op("vector", f_inv, inc="dve")
                    def f_z(v, h=h, qc=qc):
                        return v.tensor_mul(
                            z_sb[:, h * 2048 + qc * 512: h * 2048 + (qc + 1) * 512],
                            psC[:, :], invs_sb[:, :])
                    st8["z_rd"] = em.op("vector", f_z, inc="dve")
                    if not DBG_NO_QUANT:
                        quant_pump(1)

                st8[("att_done", h)] = em.cnt["pe"]
                while pending:
                    pending.pop(0)()
                if h + 2 < HL:
                    load_head(h + 2)
            sb_close("kq2")
            sb_close("sqh")
            sb_close("wqkbuf")

        def emit_wo_loads():
            wo_sb = sb_open("wo_sb", [128, HL * 2048], BF16)
            em.wait("gpsimd", "pe", st8[("qk_pe", HL - 1)])
            last_ld = 0
            for fc in range(HL):
                def f(g, fc=fc):
                    return g.dma_start(wo_sb[:, fc * 2048:(fc + 1) * 2048], wo[fc][:, :])
                last_ld = em.op("gpsimd", f, inc="dgp", amt=16)
            return wo_sb, last_ld

        def emit_wo(wo_sb, last_ld):
            wo_stage = sb_open("wo_stage", [128, 2 * 2048], F32)
            stg = {}
            store = {}
            em.wait("tensor", "dgp", last_ld)
            order = [0, 1, 8, 9, 2, 3, 10, 11, 4, 5, 12, 13, 6, 7, 14, 15]
            for idx, tt in enumerate(order):
                slot = idx % 2
                stt_ = wo_stage[:, slot * 2048:(slot + 1) * 2048]
                for dc in range(4):
                    bank = (psA, psB)[dc % 2]
                    em.wait("tensor", "dve",
                            stg.get((idx, dc - 2), 0) or stg.get((idx - 1, dc + 2), 0))
                    p_wo = None
                    for fc in range(HL):
                        def f_mm(t, tt=tt, dc=dc, fc=fc, bank=bank):
                            return t.matmul(
                                bank[:, :],
                                z_sb[:, fc * 2048 + tt * 128: fc * 2048 + (tt + 1) * 128],
                                wo_sb[:, fc * 2048 + dc * 512: fc * 2048 + (dc + 1) * 512],
                                start=(fc == 0), stop=(fc == HL - 1))
                        p_wo = em.op("tensor", f_mm, inc=("pe" if fc == HL - 1 else None))
                    em.wait("vector", "pe", p_wo)
                    em.wait("vector", "dwo%d" % slot, store.get(idx - 2, 0))
                    def f_ev(v, dc=dc, stt_=stt_, bank=bank):
                        return v.tensor_copy(stt_[:, dc * 512:(dc + 1) * 512], bank[:, :])
                    stg[(idx, dc)] = em.op("vector", f_ev, inc="dve")
                em.wait("sync", "dve", stg[(idx, 3)])
                rank, q = tt // 8, tt % 8
                pos = (q // 2) * 512 + rank * 256 + (q % 2) * 128
                def f_st(sy, pos=pos, stt_=stt_):
                    return sy.dma_start(wo_part[pos:pos + 128, :], stt_)
                store[idx] = em.op("sync", f_st, inc="dwo%d" % slot, amt=16)
                if idx % 4 == 3:
                    em.wait("gpsimd", "dwo0", store[idx - 1])
                    em.wait("gpsimd", "dwo1", store[idx])
                    c = idx // 4
                    def f_rs(g, c=c):
                        return g.collective_compute(
                            "ReduceScatter", ALU.add,
                            replica_groups=[[0, 1], [2, 3], [4, 5], [6, 7]],
                            ins=[wo_part[c * 512:(c + 1) * 512, :].opt()],
                            outs=[attn_out[c * 256:(c + 1) * 256, :].opt()])
                    em.op("gpsimd", f_rs, inc="ccrs")
            sb_close("wo_stage")

        def emit_ln2():
            xn2T = sb_open("xn2T", [128, 16 * 1024], BF16, side="right")
            abuf = sb_open("abuf", [128, 2 * 2048], F32)
            rbuf = sb_open("rbuf", [128, 2 * 2048], F32)
            sqx2 = sb_open("sqx2", [128, 2048], F32)
            nb2 = sb_open("nb2", [128, 2 * 2048], BF16)
            ln_done = {}
            trpe = {}
            for tl in range(8):
                em.wait("sync", "ccrs", tl // 2 + 1)
                slot = tl % 2
                at = abuf[:, slot * 2048:(slot + 1) * 2048]
                rt = rbuf[:, slot * 2048:(slot + 1) * 2048]
                em.wait("sync", "dve", ln_done.get(tl - 2, 0))
                def f_a(sy, tl=tl, at=at):
                    return sy.dma_start(at, attn_out[tl * 128:(tl + 1) * 128, :])
                em.op("sync", f_a, inc="dsy", amt=16)
                def f_r(sy, tl=tl, rt=rt):
                    return sy.dma_start(rt, x_res[tl * 128:(tl + 1) * 128, :])
                d_r = em.op("sync", f_r, inc="dsy", amt=16)
                em.wait("vector", "dsy", d_r)
                def f_add(v, at=at, rt=rt):
                    return v.tensor_add(rt, rt, at)
                d_x1 = em.op("vector", f_add, inc="dve")
                em.wait("vector", "dve", d_x1)
                em.wait("sync", "dve", d_x1)
                def f_st(sy, tl=tl, rt=rt):
                    return sy.dma_start(x1d[tl * 128:(tl + 1) * 128, :], rt)
                em.op("sync", f_st, inc="dsy", amt=16)
                nb = nb2[:, slot * 2048:(slot + 1) * 2048]
                d_n = emit_ln(rt, nb, sqx2[:, :], pe_war=trpe.get(tl - 2, 0))
                ln_done[tl] = d_n
                em.wait("tensor", "dve", d_n)
                _, p_tr = emit_transpose_block(nb, xn2T, tl * 128, 1024, 16)
                trpe[tl] = p_tr
            sb_close("nb2")
            sb_close("sqx2")
            sb_close("rbuf")
            sb_close("abuf")
            return xn2T

        def emit_ffn(xn2T):
            h_sb = sb_open("h_sb", [128, 64 * 512], BF16)
            wgu_t = sb_open("wgu_t", [128, 2 * 2 * 2048], BF16)
            sg_buf = sb_open("sg_buf", [128, 2 * 512], F32)
            h2_buf = sb_open("h2_buf", [128, 2 * 512], F32)
            wp_t = sb_open("wp_t", [128, 2 * 8 * 512], BF16)
            x1buf = sb_open("x1buf", [128, 4 * 2048], F32)

            em.wait("sync", "ccgu", em.cnt["ccgu"])
            em.wait("sync", "ccp", em.cnt["ccp"])

            def load_gu(gpc):
                th2, pc = divmod(gpc, 64)
                slot = gpc % 2
                base = slot * 2 * 2048
                em.wait("sync", "pe", st8.get(("gu_pe", gpc - 2), 0))
                last = 0
                for gi, hc in ((0, pc), (1, pc + 64)):
                    o_ap = bass.AP(wgu_t, base + gi * 2048,
                                   [[2 * 2 * 2048, 128], [128, 16], [1, 128]])
                    i_ap = bass.AP(cc_gu_out, hc * 2048 * 128,
                                   [[2048, 128], [128, 16], [1, 128]])
                    def f(sy, o_ap=o_ap, i_ap=i_ap):
                        return sy.dma_start(o_ap, i_ap)
                    last = em.op("sync", f, inc="dgu%d" % slot, amt=16)
                st8[("gu_ld", gpc)] = last

            def load_wp(th2, dc, r, slot):
                em.wait("sync", "pe", st8.get(("wp_war", slot), 0))
                o_ap = bass.AP(wp_t, slot * 8 * 512,
                               [[2 * 8 * 512, 128], [512, 8], [1, 512]])
                i_ap = bass.AP(cc_p_out, (r * 4096 + dc * 1024) * 512,
                               [[512, 128], [128 * 512, 8], [1, 512]])
                def f(sy, o_ap=o_ap, i_ap=i_ap):
                    return sy.dma_start(o_ap, i_ap)
                return em.op("sync", f, inc="dwp%d" % slot, amt=16)

            load_gu(0)
            for th2 in range(2):
                def emit_ssq(pc):
                    em.wait("tensor", "dve", st8[("h2", pc)])
                    if pc == 0:
                        em.wait("tensor", "dve", st8.get("rstd_rd", 0))
                    def f_ssq(t, pc=pc):
                        return t.matmul(psE[:, :], ones128f[:, :],
                                        h2_buf[:, (pc % 2) * 512:(pc % 2 + 1) * 512],
                                        start=(pc == 0), stop=(pc == 63))
                    st8[("ssq_pe", pc)] = em.op("tensor", f_ssq, inc="pe")

                for pc in range(64):
                    gpc = th2 * 64 + pc
                    slot = gpc % 2
                    base = slot * 2 * 2048
                    if pc == 0:
                        for j in range(4):
                            em.wait("tensor", "dve", st8.get(("prj_war", j), 0))
                    if gpc + 1 < 128:
                        load_gu(gpc + 1)
                    em.wait("tensor", "dgu%d" % (gpc % 2), st8[("gu_ld", gpc)])
                    gbank = (psA, psB)[gpc % 2]
                    ubank = (psC, psD)[gpc % 2]
                    em.wait("tensor", "act", st8.get(("silu_war", gbank is psA), 0))
                    em.wait("tensor", "dve", st8.get(("h_war", ubank is psC), 0))
                    p_g = None
                    for dc in range(16):
                        def f_g(t, dc=dc, th2=th2, gbank=gbank, base=base):
                            return t.matmul(
                                gbank[:, :],
                                wgu_t[:, base + dc * 128: base + (dc + 1) * 128],
                                xn2T[:, dc * 1024 + th2 * 512: dc * 1024 + (th2 + 1) * 512],
                                start=(dc == 0), stop=(dc == 15))
                        p_g = em.op("tensor", f_g, inc=("pe" if dc == 15 else None))
                    p_u = None
                    for dc in range(16):
                        def f_u(t, dc=dc, th2=th2, ubank=ubank, base=base):
                            return t.matmul(
                                ubank[:, :],
                                wgu_t[:, base + 2048 + dc * 128: base + 2048 + (dc + 1) * 128],
                                xn2T[:, dc * 1024 + th2 * 512: dc * 1024 + (th2 + 1) * 512],
                                start=(dc == 0), stop=(dc == 15))
                        p_u = em.op("tensor", f_u, inc=("pe" if dc == 15 else None))
                    st8[("gu_pe", gpc)] = p_u
                    if pc > 0:
                        emit_ssq(pc - 1)
                    em.wait("scalar", "pe", p_g)
                    em.wait("scalar", "dve", st8.get(("sg_war", pc % 2), 0))
                    def f_silu(s, gbank=gbank, pc=pc):
                        return s.activation(sg_buf[:, (pc % 2) * 512:(pc % 2 + 1) * 512],
                                            gbank[:, :], AF.Silu)
                    a_silu = em.op("scalar", f_silu, inc="act")
                    st8[("silu_war", gbank is psA)] = a_silu
                    em.wait("vector", "act", a_silu)
                    em.wait("vector", "pe", p_u)
                    em.wait("vector", "pe", st8.get("proj_done", 0))
                    hsl = h_sb[:, pc * 512:(pc + 1) * 512]
                    def f_h(v, hsl=hsl, ubank=ubank, pc=pc):
                        return v.tensor_mul(hsl, sg_buf[:, (pc % 2) * 512:(pc % 2 + 1) * 512],
                                            ubank[:, :])
                    d_h = em.op("vector", f_h, inc="dve")
                    st8[("h_war", ubank is psC)] = d_h
                    st8[("sg_war", pc % 2)] = d_h
                    em.wait("vector", "pe", st8.get(("ssq_pe", pc - 2), 0))
                    def f_h2(v, hsl=hsl, pc=pc):
                        return v.tensor_mul(h2_buf[:, (pc % 2) * 512:(pc % 2 + 1) * 512],
                                            hsl, hsl)
                    st8[("h2", pc)] = em.op("vector", f_h2, inc="dve")
                emit_ssq(63)
                em.wait("scalar", "pe", st8[("ssq_pe", 63)])
                def f_lnS(s):
                    return s.activation(srow[0:1, 0:512], psE[0:1, :], AF.Ln,
                                        bias=epsb[0:1, 1:2], scale=1.0 / HID)
                a_lnS = em.op("scalar", f_lnS, inc="act")
                em.wait("scalar", "act", a_lnS)
                def f_exS(s):
                    return s.activation(srow[0:1, 512:1024], srow[0:1, 0:512],
                                        AF.Exp, scale=-0.5)
                a_r = em.op("scalar", f_exS, inc="act")
                em.wait("tensor", "act", a_r)
                for j in range(4):
                    def f_rc(t, j=j):
                        return t.matmul(psE[:, j:j + 1],
                                        srow[0:1, 512 + j * 128: 512 + (j + 1) * 128],
                                        ones_row[0:1, 0:1], start=True, stop=True)
                    p_rc = em.op("tensor", f_rc, inc=("pe" if j == 3 else None))
                em.wait("vector", "pe", p_rc)
                def f_evR(v):
                    return v.tensor_copy(stats[:, 8:12], psE[:, 0:4])
                d_evR = em.op("vector", f_evR, inc="dve")
                st8["rstd_rd"] = d_evR
                x1_ld = {}
                for j in range(4):
                    tt = th2 * 4 + j
                    em.wait("sync", "dve", st8.get(("x1_rd", j), 0))
                    def f_x1(sy, tt=tt, j=j):
                        return sy.dma_start(x1buf[:, j * 2048:(j + 1) * 2048],
                                            x1d[tt * 128:(tt + 1) * 128, :])
                    x1_ld[j] = em.op("sync", f_x1, inc="dsy", amt=16)
                em.wait("tensor", "dve", st8[("h2", 63)])
                banks4 = (psA, psB, psC, psD)
                for dc in range(4):
                    wp_ld = {0: load_wp(th2, dc, 0, 0)}
                    p_r = 0
                    for r in range(8):
                        slot = r % 2
                        if r + 1 < 8:
                            wp_ld[r + 1] = load_wp(th2, dc, r + 1, (r + 1) % 2)
                        em.wait("tensor", "dwp%d" % (r % 2), wp_ld[r])
                        for rc in range(8):
                            hc = r * 8 + rc
                            for j in range(4):
                                if hc == 0:
                                    em.wait("tensor", "dve",
                                            st8.get(("prj_war", j), 0))
                                def f_mm(t, hc=hc, j=j, rc=rc, slot=slot, th2=th2):
                                    tl = th2 * 4 + j - th2 * 4
                                    return t.matmul(
                                        banks4[j][:, :],
                                        h_sb[:, hc * 512 + j * 128: hc * 512 + (j + 1) * 128],
                                        wp_t[:, slot * 8 * 512 + rc * 512:
                                             slot * 8 * 512 + (rc + 1) * 512],
                                        start=(hc == 0), stop=(hc == 63))
                                inc = "pe" if (rc == 7 and j == 3) else None
                                p_r = em.op("tensor", f_mm, inc=inc)
                        st8[("wp_war", slot)] = p_r
                    for j in range(4):
                        em.wait("vector", "pe", p_r)
                        em.wait("vector", "dsy", x1_ld[3])
                        em.wait("vector", "dve", st8["rstd_rd"])
                        def f_ev(v, j=j, dc=dc):
                            sl = x1buf[:, j * 2048 + dc * 512: j * 2048 + (dc + 1) * 512]
                            return v.scalar_tensor_tensor(sl, banks4[j][:, :],
                                                          stats[:, 8 + j:9 + j], sl,
                                                          ALU.mult, ALU.add)
                        st8[("prj_war", j)] = em.op("vector", f_ev, inc="dve")
                for j in range(4):
                    tt = th2 * 4 + j
                    em.wait("sync", "dve", st8[("prj_war", j)])
                    st8[("x1_rd", j)] = st8[("prj_war", j)]
                    def f_out(sy, tt=tt, j=j):
                        return sy.dma_start(out_t[tt * 128:(tt + 1) * 128, :],
                                            x1buf[:, j * 2048:(j + 1) * 2048])
                    em.op("sync", f_out, inc="dsy", amt=16)
                st8["proj_done"] = st8.get(("wp_war", 0), 0) or 0
                st8["proj_done"] = max(st8.get("proj_done", 0),
                                       st8.get(("wp_war", 1), 0) or 0)

        emit_consts()
        emit_ln1_v()
        sb_close("xnbuf")
        sb_close("wv_sb")
        sb_close("sqx")
        sb_close("xin")
        if stage in ("heads", "wo", "ln2", "full"):
            z_sb = sb_open("z_sb", [128, HL * 2048], BF16, side="right")
            qwbf = sb_open("qwbf", [128, 2 * 2048], BF16, side="right")
            qabs = sb_open("qabs", [128, 2 * 2048], BF16, side="right")
            qsgn = sb_open("qsgn", [128, 2 * 2048], BF16, side="right")
            qg = sb_open("qg", [128, 2 * 2048], BF16, side="right")
            qsbf = sb_open("qsbf", [128, 512], BF16, side="right")
            em.fence()
            emit_heads()
            sb_close("xn1T")
            while qs["done"] < len(q_tiles):
                quant_step()
            quant_gathers()
            wo_sb, wo_last_ld = emit_wo_loads()
        if stage in ("wo", "ln2", "full"):
            em.fence()
            emit_wo(wo_sb, wo_last_ld)
            for nm in ("qsbf", "qg", "qsgn", "qabs", "qwbf"):
                sb_close(nm)
            sb_close("wo_sb")
            sb_close("v_sb")
            sb_close("z_sb")
        xn2T = None
        if stage in ("ln2", "full"):
            em.fence()
            xn2T = emit_ln2()
        if stage == "full":
            em.fence()
            emit_ffn(xn2T)
            for nm in ("x1buf", "wp_t", "h2_buf", "sg_buf", "wgu_t", "h_sb"):
                sb_close(nm)
            sb_close("xn2T")

        tap_outs = []
        for name, key, ncols in taps:
            tensor = reg[key]
            o = nc.dram_tensor("tap_" + name, [128, ncols], tensor.dtype,
                               kind="ExternalOutput")
            tap_outs.append((o, tensor, ncols))
        if tap_outs:
            em.fence(("sync",), exclude=())
            for o, tensor, ncols in tap_outs:
                def f(sy, o=o, tensor=tensor, ncols=ncols):
                    return sy.dma_start(o[:, :], tensor[:, 0:ncols])
                em.op("sync", f, inc="dsy", amt=16)

        def f_fin(sy):
            return sy.wait_ge(sems["dsy"], em.cnt["dsy"])
        em.q["sync"].append(f_fin)
        def f_fin2(g):
            return g.wait_ge(sems["dgp"], em.cnt["dgp"])
        em.q["gpsimd"].append(f_fin2)

        with nc.Block() as block:
            @block.tensor
            def _(t):
                for f in em.q["tensor"]:
                    f(t)

            @block.vector
            def _(v):
                for f in em.q["vector"]:
                    f(v)

            @block.scalar
            def _(s):
                for f in em.q["scalar"]:
                    f(s)

            @block.gpsimd
            def _(g):
                for f in em.q["gpsimd"]:
                    f(g)

            @block.sync
            def _(sy):
                for f in em.q["sync"]:
                    f(sy)

        for name in list(reversed(list(_open_cms))):
            sb_close(name)

    return nc


def make_inputs(L0_attn_x, L0_attn_W_K, L0_attn_W_Q, L0_attn_W_V, L0_attn_W_O,
                L0_ffn_w_gate_up, L0_ffn_w_proj):
    x = np.asarray(L0_attn_x, np.float32)
    W_K = np.asarray(L0_attn_W_K, np.float32)
    W_Q = np.asarray(L0_attn_W_Q, np.float32)
    W_V = np.asarray(L0_attn_W_V, np.float32)
    W_O = np.asarray(L0_attn_W_O, np.float32)
    wgu_f = np.asarray(L0_ffn_w_gate_up, np.float32)
    wp_f = np.asarray(L0_ffn_w_proj, np.float32)
    woT = np.ascontiguousarray(W_O.T)
    in_maps = []
    for c in range(NCORES):
        b, par = c // 2, c % 2
        hr = slice(par * HL, (par + 1) * HL)
        wq_t = W_Q[hr].transpose(0, 2, 1).reshape(HL, 16, 128, 128)
        wk_t = W_K[hr].transpose(0, 2, 1).reshape(HL, 16, 128, 128)
        wqk_h = np.ascontiguousarray(np.stack([wq_t, wk_t]))
        wv_t = np.ascontiguousarray(
            W_V[hr].transpose(0, 2, 1).reshape(HL, 16, 128, 128)
            .transpose(1, 2, 0, 3)).reshape(16, 128, HL * 128)
        wo_my = np.ascontiguousarray(woT[par * 1024:(par + 1) * 1024]).reshape(HL, 128, D)
        wgu_sh = np.ascontiguousarray(wgu_f[c * 2048:(c + 1) * 2048, :].T).reshape(16, 128, 2048)
        wp_sh = np.ascontiguousarray(wp_f[:, c * 1024:(c + 1) * 1024].T).reshape(8, 128, 2048)
        in_maps.append({
            "x_full": np.ascontiguousarray(x[b]),
            "x_res": np.ascontiguousarray(x[b, par * TOK:(par + 1) * TOK]),
            "wqk": wqk_h,
            "wv": wv_t,
            "wo": wo_my,
            "wgu": wgu_sh,
            "wp": wp_sh,
        })
    return in_maps


def kernel(L0_attn_x, L0_attn_W_K, L0_attn_W_Q, L0_attn_W_V, L0_attn_W_O,
           L0_ffn_w_gate_up, L0_ffn_w_proj):
    in_maps = make_inputs(L0_attn_x, L0_attn_W_K, L0_attn_W_Q, L0_attn_W_V,
                          L0_attn_W_O, L0_ffn_w_gate_up, L0_ffn_w_proj)
    nc = build("full")
    res = run_bass_kernel_spmd(nc, in_maps, core_ids=list(range(NCORES)))
    out = np.empty((B, S, D), np.float32)
    for c in range(NCORES):
        b, par = c // 2, c % 2
        out[b, par * TOK:(par + 1) * TOK] = res.results[c]["out"]
    return out


def build_with_taps(stage, taps, **kw):
    return build(stage, taps)


# version 15 (final)
# speedup vs baseline: 1.2917x; 1.0048x over previous
import sys

sys.path.insert(0, "/opt/trn_rl_repo")

import numpy as np
import concourse.bass as bass
import concourse.mybir as mybir
from concourse.bass_utils import run_bass_kernel_spmd

F32 = mybir.dt.float32
BF16 = mybir.dt.bfloat16
AF = mybir.ActivationFunctionType
ALU = mybir.AluOpType
AX = mybir.AxisListType

NCORES = 8
B, S, D = 4, 2048, 2048
H, DH = 16, 128
HL = H // 2
TOK = S // 2
HID = 4 * D
INV_SQRT_DH = float(1.0 / np.sqrt(np.float32(DH)))
LN_EPS = 1e-5
RMS_EPS = float(np.finfo(np.float32).eps)
THETA = 0.5 + 2.0 ** -10

ENGS = ("tensor", "vector", "scalar", "gpsimd", "sync")


class Em:

    def __init__(self, sems):
        self.sems = sems
        self.q = {e: [] for e in ENGS}
        self.cnt = {k: 0 for k in sems}
        self._lastwait = {}

    def op(self, eng, fn, inc=None, amt=1):
        if inc is None:
            self.q[eng].append(fn)
            return None
        self.cnt[inc] += amt
        sem = self.sems[inc]
        self.q[eng].append(lambda E, f=fn, s=sem, a=amt: f(E).then_inc(s, a))
        return self.cnt[inc]

    def wait(self, eng, sem_name, val):
        if not val:
            return
        key = (eng, sem_name)
        if self._lastwait.get(key, 0) >= val:
            return
        assert val <= self.cnt[sem_name], \
            f"wait {sem_name}>={val} but only {self.cnt[sem_name]} emitted"
        self._lastwait[key] = val
        s = self.sems[sem_name]
        self.q[eng].append(lambda E, s=s, v=val: E.wait_ge(s, v))

    def fence(self, engs=ENGS, exclude=("ccgu", "ccp", "ccrs", "dgp", "dq0", "dq1",
                                         "dwqk0", "dwqk1", "dgu0", "dgu1",
                                         "dwp0", "dwp1", "dwv", "dx0", "dx1")):
        snap = dict(self.cnt)
        for e in engs:
            for k, v in snap.items():
                if v and k not in exclude:
                    self.wait(e, k, v)


DBG_NO_V = False
DBG_NO_QUANT = False
DBG_LN_TILES = 16


def build(stage="full", taps=()):
    nc = bass.Bass("TRN2", target_bir_lowering=False)

    x_full = nc.dram_tensor("x_full", [S, D], F32, kind="ExternalInput")
    x_res = nc.dram_tensor("x_res", [TOK, D], F32, kind="ExternalInput")
    wqk = nc.dram_tensor("wqk", [2, HL, 16, 128, 128], F32, kind="ExternalInput")
    wv = nc.dram_tensor("wv", [16, 128, HL * 128], F32, kind="ExternalInput")
    wo = nc.dram_tensor("wo", [HL, 128, D], F32, kind="ExternalInput")
    wgu = nc.dram_tensor("wgu", [16, 128, 2048], F32, kind="ExternalInput")
    wp = nc.dram_tensor("wp", [8, 128, 2048], F32, kind="ExternalInput")
    out_t = nc.dram_tensor("out", [TOK, D], F32, kind="ExternalOutput")

    cc_gu_in = nc.dram_tensor("cc_gu_in", [2048 * 16, 128], BF16)
    cc_gu_out = nc.dram_tensor("cc_gu_out", [NCORES * 2048 * 16, 128], BF16,
                               addr_space="Shared")
    cc_p_in = nc.dram_tensor("cc_p_in", [4096, 512], BF16)
    cc_p_out = nc.dram_tensor("cc_p_out", [NCORES * 4096, 512], BF16,
                              addr_space="Shared")
    wo_part = nc.dram_tensor("wo_part", [S, D], F32)
    attn_out = nc.dram_tensor("attn_out", [TOK, D], F32)
    x1d = nc.dram_tensor("x1d", [TOK, D], F32)

    from contextlib import ExitStack
    stack = ExitStack()
    
    ps = lambda name, shape, dt: stack.enter_context(nc.psum_tensor(name, shape, dt))

    _open_cms = {}
    reg = {}

    def sb_open(name, shape, dt, side=None):
        cm = nc.sbuf_tensor(name, shape, dt, side=side) if side else nc.sbuf_tensor(name, shape, dt)
        h = cm.__enter__()
        _open_cms[name] = cm
        reg[name] = h
        return h

    def sb_close(name):
        _open_cms.pop(name).__exit__(None, None, None)

    def sbr(name, shape, dt):
        h = stack.enter_context(nc.sbuf_tensor(name, shape, dt))
        reg[name] = h
        return h

    with stack:
        ident = sbr("ident", [128, 128], BF16)
        masks = sbr("masks", [128, 4 * 512], BF16)
        bones = sbr("bones", [128, 128], BF16)
        ones_col = sbr("ones_col", [128, 1], BF16)
        ones128 = sbr("ones128", [128, 128], BF16)
        ones128f = sbr("ones128f", [128, 128], F32)
        ones_col_f = sbr("ones_col_f", [128, 1], F32)
        epsb = sbr("epsb", [128, 2], F32)
        ones_row = sbr("ones_row", [1, 128], F32)
        stats = sbr("stats", [128, 16], F32)
        srow = sbr("srow", [128, 1024], F32)
        bias_h = sbr("bias_h", [128, 2], F32)
        u_sb = sbr("u_sb", [128, 3 * 512], BF16)
        kq_sb = sbr("kq_sb", [128, 2 * 2048], BF16)
        invs_sb = sbr("invs_sb", [128, 512], F32)

        psA = ps("psA", [128, 512], F32)
        psB = ps("psB", [128, 512], F32)
        psC = ps("psC", [128, 512], F32)
        psD = ps("psD", [128, 512], F32)
        psE = ps("psE", [128, 512], F32)
        psF = ps("psF", [128, 512], F32)
        psG = ps("psG", [128, 512], F32)
        psT = ps("psT", [128, 1024], BF16)

        qwbf = qabs = qsgn = qg = qsbf = None

        v_sb = sb_open("v_sb", [128, 16 * 1024], BF16)
        xn1T = sb_open("xn1T", [128, 16 * 2048], BF16)
        z_sb = None
        xin = sb_open("xin", [128, 2 * 2048], F32)
        sqx = sb_open("sqx", [128, 2048], F32)
        wv_sb = sb_open("wv_sb", [128, 16 * 1024], BF16)
        xnbuf = sb_open("xnbuf", [128, 2 * 2048], BF16)

        sems = {}
        for name in ("pe", "dve", "act", "pool", "dsy", "dgp", "ccgu", "ccp", "ccrs",
                     "dx0", "dx1", "dwv", "dwqk0", "dwqk1", "dq0", "dq1",
                     "dgu0", "dgu1", "dwp0", "dwp1", "dwo0", "dwo1"):
            sems[name] = stack.enter_context(nc.semaphore(name))
        em = Em(sems)
        st8 = {}

        def emit_consts():
            def f(g):
                g.memset(ident[:, :], 1.0)
                g.affine_select(ident[:, :], ident[:, :], [[1, 128]], ALU.is_equal,
                                0.0, base=0, channel_multiplier=-1)
                g.memset(masks[:, :], 1.0)
                for j in range(4):
                    g.affine_select(masks[:, j * 512:(j + 1) * 512],
                                    masks[:, j * 512:(j + 1) * 512],
                                    [[1, 512]], ALU.is_ge, 0.0,
                                    base=-128 * j, channel_multiplier=-1)
                g.memset(bones[:, :], 0.0)
                g.memset(bones[0:64, 0:64], 1.0)
                g.memset(bones[64:128, 64:128], 1.0)
                g.memset(ones_col[:, :], 1.0)
                g.memset(ones128[:, :], 1.0)
                g.memset(ones128f[:, :], 1.0)
                g.memset(ones_col_f[:, :], 1.0)
                g.memset(epsb[:, 0:1], LN_EPS)
                g.memset(epsb[:, 1:2], RMS_EPS)
                return g.memset(ones_row[:, :], 1.0)
            c = em.op("gpsimd", f, inc="pool")
            for e in ("tensor", "vector", "scalar"):
                em.wait(e, "pool", c)

        q_tiles = [("gu", i) for i in range(16)] + [("p", i) for i in range(8)]
        qs = {"done": 0, "issued": 0}
        qpend = []

        def quant_substeps(i):
            kind, idx = q_tiles[i]
            slot = i % 2
            src_t = wgu[idx] if kind == "gu" else wp[idx]
            wb = qwbf[:, slot * 2048:(slot + 1) * 2048]
            ab = qabs[:, slot * 2048:(slot + 1) * 2048]
            sg = qsgn[:, slot * 2048:(slot + 1) * 2048]
            gg = qg[:, slot * 2048:(slot + 1) * 2048]
            steps = []

            def s_load():
                em.wait("gpsimd", "dve", qs.get(("qdve", i - 2, 3), 0))
                def f_ld(g):
                    return g.dma_start(wb, src_t[:, :])
                qs[("ld", i)] = em.op("gpsimd", f_ld, inc="dq%d" % slot, amt=16)
            steps.append(s_load)

            def s_abs():
                em.wait("scalar", "dq%d" % slot, qs[("ld", i)])
                em.wait("scalar", "dve", qs.get(("qdve", i - 2, 3), 0))
                def f_abs(s):
                    return s.activation(ab, wb, AF.Abs)
                qs[("abs", i)] = em.op("scalar", f_abs, inc="act")
                def f_sgn(s):
                    return s.activation(sg, wb, AF.Sign)
                qs[("sgn", i)] = em.op("scalar", f_sgn, inc="act")
            steps.append(s_abs)

            def s_chunk(cs):
                def go():
                    if cs == 0:
                        em.wait("vector", "dgp", qs.get(("out", i - 2), 0))
                    em.wait("tensor", "act", qs[("abs", i)])
                    em.wait("tensor", "dve", qs.get(("sbf", i, cs - 1), 0) or
                            qs.get(("sbf", i - 1, 3), 0))
                    def f_mm(t, cs=cs):
                        return t.matmul(psG[:, :], bones[:, :],
                                        ab[:, cs * 512:(cs + 1) * 512],
                                        start=True, stop=True)
                    p_mm = em.op("tensor", f_mm, inc="pe")
                    em.wait("vector", "pe", p_mm)
                    def f_sbf(v):
                        return v.tensor_scalar(qsbf[:, :], psG[:, :], 1.0 / 64.0, 1e-8,
                                               ALU.mult, ALU.max)
                    d_sbf = em.op("vector", f_sbf, inc="dve")
                    qs[("sbf", i, cs)] = d_sbf
                    em.wait("vector", "dve", d_sbf)
                    em.wait("vector", "act", qs[("sgn", i)])
                    def f_g(v, cs=cs):
                        return v.scalar_tensor_tensor(gg[:, cs * 512:(cs + 1) * 512],
                                                      qsbf[:, :], THETA,
                                                      ab[:, cs * 512:(cs + 1) * 512],
                                                      ALU.mult, ALU.is_lt)
                    em.op("vector", f_g, inc="dve")
                    def f_m(v, cs=cs):
                        sl = gg[:, cs * 512:(cs + 1) * 512]
                        return v.tensor_mul(sl, sl, sg[:, cs * 512:(cs + 1) * 512])
                    qs[("m", i)] = em.op("vector", f_m, inc="dve")
                    def f_wh(v, cs=cs):
                        sl = gg[:, cs * 512:(cs + 1) * 512]
                        return v.tensor_mul(sl, sl, qsbf[:, :])
                    qs[("qdve", i, cs)] = em.op("vector", f_wh, inc="dve")
                return go
            for cs in range(4):
                steps.append(s_chunk(cs))

            def s_out():
                em.wait("gpsimd", "dve", qs[("qdve", i, 3)])
                if kind == "gu":
                    o_ap = bass.AP(cc_gu_in, idx * 128,
                                   [[2048, 128], [262144, 16], [1, 128]])
                    i_ap = bass.AP(qg, slot * 2048, [[2 * 2048, 128], [128, 16], [1, 128]])
                else:
                    o_ap = bass.AP(cc_p_in, idx * 128 * 512,
                                   [[512, 128], [8 * 128 * 512, 4], [1, 512]])
                    i_ap = bass.AP(qg, slot * 2048, [[2 * 2048, 128], [512, 4], [1, 512]])
                def f_out(g):
                    return g.dma_start(o_ap, i_ap)
                qs[("out", i)] = em.op("gpsimd", f_out, inc="dgp", amt=16)
                qs["done"] = i + 1
            steps.append(s_out)
            return steps

        def quant_pump(n):
            for _ in range(n):
                if not qpend and qs["issued"] < len(q_tiles):
                    qpend.extend(quant_substeps(qs["issued"]))
                    qs["issued"] += 1
                if qpend:
                    qpend.pop(0)()

        def quant_step():
            if DBG_NO_QUANT:
                return
            if qs["issued"] < len(q_tiles) or qpend:
                quant_pump(9)

        def quant_gathers():
            em.wait("gpsimd", "dgp", qs[("out", len(q_tiles) - 1)])
            def f_gu(g):
                return g.collective_compute(
                    "AllGather", ALU.bypass, replica_groups=[list(range(NCORES))],
                    ins=[cc_gu_in.ap().opt()], outs=[cc_gu_out.ap().opt()])
            em.op("gpsimd", f_gu, inc="ccgu")
            def f_p(g):
                return g.collective_compute(
                    "AllGather", ALU.bypass, replica_groups=[list(range(NCORES))],
                    ins=[cc_p_in.ap().opt()], outs=[cc_p_out.ap().opt()])
            em.op("gpsimd", f_p, inc="ccp")

        def emit_ln(xtile, outbf, sqbuf, wait_dsy=0, dsem="dsy", pe_war=0):
            em.wait("vector", dsem, wait_dsy)
            def f_s1(v):
                return v.reduce_sum(stats[:, 0:1], xtile, axis=AX.X)
            em.op("vector", f_s1, inc="dve")
            def f_sq(v):
                return v.tensor_mul(sqbuf, xtile, xtile)
            d_sq = em.op("vector", f_sq, inc="dve")
            em.wait("vector", "dve", d_sq)
            def f_s2(v):
                return v.reduce_sum(stats[:, 1:2], sqbuf, axis=AX.X)
            d_s2 = em.op("vector", f_s2, inc="dve")
            st8["sqx_rd"] = d_s2
            def f_mean(v):
                return v.tensor_scalar(stats[:, 2:3], stats[:, 0:1], 1.0 / D, None, ALU.mult)
            d_mean = em.op("vector", f_mean, inc="dve")
            em.wait("vector", "dve", d_mean)
            def f_m2(v):
                return v.tensor_mul(stats[:, 3:4], stats[:, 2:3], stats[:, 2:3])
            d_m2 = em.op("vector", f_m2, inc="dve")
            em.wait("vector", "dve", d_m2)
            def f_var(v):
                return v.scalar_tensor_tensor(stats[:, 4:5], stats[:, 1:2], 1.0 / D,
                                              stats[:, 3:4], ALU.mult, ALU.subtract)
            d_var = em.op("vector", f_var, inc="dve")
            em.wait("scalar", "dve", d_var)
            def f_ln(s):
                return s.activation(stats[:, 5:6], stats[:, 4:5], AF.Ln, bias=epsb[:, 0:1])
            a_ln = em.op("scalar", f_ln, inc="act")
            em.wait("scalar", "act", a_ln)
            def f_ex(s):
                return s.activation(stats[:, 6:7], stats[:, 5:6], AF.Exp, scale=-0.5)
            a_r = em.op("scalar", f_ex, inc="act")
            em.wait("vector", "act", a_r)
            em.wait("vector", "pe", pe_war)
            def f_norm(v):
                return v.tensor_scalar(outbf, xtile, stats[:, 2:3], stats[:, 6:7],
                                       ALU.subtract, ALU.mult)
            return em.op("vector", f_norm, inc="dve")

        def emit_transpose_block(srcbf, dst, col0, stride, n=16):
            last_pe = 0
            for grp in range(0, n, 8):
                cnt = min(8, n - grp)
                em.wait("tensor", "dve", st8.get("tr_evac", 0))
                for j in range(cnt):
                    def f_tr(t, j=j, grp=grp):
                        return t.transpose(psT[:, j * 128:(j + 1) * 128],
                                           srcbf[:, (grp + j) * 128:(grp + j + 1) * 128],
                                           ident[:, :])
                    last_pe = em.op("tensor", f_tr, inc="pe")
                em.wait("vector", "pe", last_pe)
                o_ap = bass.AP(dst, col0 + grp * stride,
                               [[dst.shape[1], 128], [stride, cnt], [1, 128]])
                i_ap = bass.AP(psT, 0, [[1024, 128], [128, cnt], [1, 128]])
                def f_ev(v, o_ap=o_ap, i_ap=i_ap):
                    return v.tensor_copy(o_ap, i_ap)
                st8["tr_evac"] = em.op("vector", f_ev, inc="dve")
            return st8["tr_evac"], last_pe

        def emit_ln1_v():
            def f_wv(g):
                i_ap = bass.AP(wv, 0, [[1024, 128], [128 * 1024, 16], [1, 1024]])
                o_ap = bass.AP(wv_sb, 0, [[16 * 1024, 128], [1024, 16], [1, 1024]])
                return g.dma_start(o_ap, i_ap)
            d_wv = em.op("gpsimd", f_wv, inc="dwv", amt=16)

            ln_done = {}
            trpe = {}
            v_ev = {}

            def emit_pe_tile(tt):
                slot = tt % 2
                nb = xnbuf[:, slot * 2048:(slot + 1) * 2048]
                em.wait("tensor", "dve", ln_done[tt])
                d_tr, p_tr = emit_transpose_block(nb, xn1T, tt * 128, 2048, 16)
                trpe[tt] = p_tr
                if DBG_NO_V:
                    return
                em.wait("tensor", "dwv", d_wv)
                em.wait("tensor", "dve", d_tr)
                for half, bank in ((0, psA), (1, psB)):
                    em.wait("tensor", "dve", v_ev.get((tt - 1, half), 0))
                    p_v = None
                    for dc in range(16):
                        def f_mm(t, tt=tt, half=half, dc=dc, bank=bank):
                            return t.matmul(
                                bank[:, :],
                                xn1T[:, dc * 2048 + tt * 128: dc * 2048 + (tt + 1) * 128],
                                wv_sb[:, dc * 1024 + half * 512: dc * 1024 + (half + 1) * 512],
                                start=(dc == 0), stop=(dc == 15))
                        p_v = em.op("tensor", f_mm, inc=("pe" if dc == 15 else None))
                    em.wait("vector", "pe", p_v)
                    def f_ev(v, tt=tt, half=half, bank=bank):
                        return v.tensor_copy(
                            v_sb[:, tt * 1024 + half * 512: tt * 1024 + (half + 1) * 512],
                            bank[:, :])
                    v_ev[(tt, half)] = em.op("vector", f_ev, inc="dve")

            for tt in range(DBG_LN_TILES):
                slot = tt % 2
                xt = xin[:, slot * 2048:(slot + 1) * 2048]
                em.wait("sync", "dve", ln_done.get(tt - 2, 0))
                def f_x(sy, tt=tt, xt=xt):
                    return sy.dma_start(xt, x_full[tt * 128:(tt + 1) * 128, :])
                d_x = em.op("sync", f_x, inc="dx%d" % slot, amt=16)
                nb = xnbuf[:, slot * 2048:(slot + 1) * 2048]
                d_n = emit_ln(xt, nb, sqx[:, :], wait_dsy=d_x, dsem="dx%d" % slot,
                              pe_war=trpe.get(tt - 2, 0))
                ln_done[tt] = d_n
                if tt > 0:
                    emit_pe_tile(tt - 1)
            emit_pe_tile(DBG_LN_TILES - 1)


        def emit_heads():
            wqkbuf = sb_open("wqkbuf", [128, 2 * 2048], BF16)
            sqh = sb_open("sqh", [128, 512], F32)
            kq2 = sb_open("kq2", [128, 2 * 2048], BF16)
            wqk_ld = {}

            def kq_base(h):
                return (kq_sb, 0) if h % 2 == 0 else (kq2, 0)

            def load_head(h):
                slot = h % 2
                em.wait("gpsimd", "pe", st8.get(("qk_pe", h - 1), 0))
                for ti in range(2):
                    o_ap = bass.AP(wqkbuf, ti * 2048,
                                   [[2 * 2048, 128], [128, 16], [1, 128]])
                    i_ap = bass.AP(wqk, ti * HL * 16 * 128 * 128 + h * 16 * 128 * 128,
                                   [[128, 128], [128 * 128, 16], [1, 128]])
                    def f(g, o_ap=o_ap, i_ap=i_ap):
                        return g.dma_start(o_ap, i_ap)
                    wqk_ld[h] = em.op("gpsimd", f, inc="dwqk%d" % slot, amt=16)

            def production_steps(h):
                if h >= HL:
                    return []
                slot = h % 2
                base = 0
                kqt, _ = kq_base(h)
                steps = []

                def grp(ti, qc, toff):
                    def go():
                        bank = psF
                        em.wait("tensor", "dwqk%d" % slot, wqk_ld[h])
                        em.wait("tensor", "dve", st8.get(("ef_war", bank is psE), 0))
                        p_qk = None
                        for dc in range(16):
                            def f_mm(t, ti=ti, qc=qc, dc=dc, bank=bank):
                                return t.matmul(
                                    bank[:, :],
                                    wqkbuf[:, base + ti * 2048 + dc * 128:
                                           base + ti * 2048 + (dc + 1) * 128],
                                    xn1T[:, dc * 2048 + qc * 512: dc * 2048 + (qc + 1) * 512],
                                    start=(dc == 0), stop=(dc == 15))
                            p_qk = em.op("tensor", f_mm, inc=("pe" if dc == 15 else None))
                        st8[("qk_pe", h)] = p_qk
                        em.wait("vector", "pe", st8.get(("att_done", h - 2), 0))
                        em.wait("vector", "pe", p_qk)
                        def f_ev(v, qc=qc, toff=toff, bank=bank, kqt=kqt):
                            return v.tensor_copy(kqt[:, toff + qc * 512: toff + (qc + 1) * 512],
                                                 bank[:, :])
                        d = em.op("vector", f_ev, inc="dve")
                        st8[("ef_war", bank is psE)] = d
                        st8[("kq_ev", h)] = d
                    return go

                for ti, toff in ((1, 0), (0, 2048)):
                    for qc in range(4):
                        steps.append(grp(ti, qc, toff))

                def norm_chunk(part, cs, mxi):
                    def go():
                        em.wait("scalar", "dve", st8[("kq_ev", h)])
                        em.wait("scalar", "pe", st8.get("nrm_pe", 0))
                        def f_sq(s, part=part, cs=cs, kqt=kqt):
                            return s.activation(
                                sqh[:, :],
                                kqt[:, part * 2048 + cs * 512: part * 2048 + (cs + 1) * 512],
                                AF.Square)
                        a_sq = em.op("scalar", f_sq, inc="act")
                        bank = psF
                        em.wait("tensor", "act", a_sq)
                        em.wait("tensor", "dve", st8.get(("ef_war", bank is psE), 0))
                        def f_mm(t, bank=bank):
                            return t.matmul(bank[:, :], ones128f[:, :], sqh[:, :],
                                            start=True, stop=True)
                        p_n = em.op("tensor", f_mm, inc="pe")
                        st8["nrm_pe"] = p_n
                        em.wait("vector", "pe", p_n)
                        def f_mx(v, mxi=mxi, bank=bank):
                            return v.reduce_max(srow[0:1, 512 + mxi:513 + mxi], bank[0:1, :],
                                                axis=AX.X)
                        d_mx = em.op("vector", f_mx, inc="dve")
                        st8[("ef_war", bank is psE)] = d_mx
                    return go

                mxi = 0
                for part in range(2):
                    for cs in range(4):
                        steps.append(norm_chunk(part, cs, mxi))
                        mxi += 1

                def finale():
                    em.wait("vector", "dve", st8.get(("ef_war", True), 0))
                    em.wait("vector", "dve", st8.get(("ef_war", False), 0))
                    def f_km(v):
                        return v.reduce_max(srow[0:1, 520:521], srow[0:1, 512:516], axis=AX.X)
                    em.op("vector", f_km, inc="dve")
                    def f_qm(v):
                        return v.reduce_max(srow[0:1, 521:522], srow[0:1, 516:520], axis=AX.X)
                    d_qm = em.op("vector", f_qm, inc="dve")
                    em.wait("vector", "dve", d_qm)
                    def f_pr(v):
                        return v.tensor_mul(srow[0:1, 522:523], srow[0:1, 520:521],
                                            srow[0:1, 521:522])
                    d_pr = em.op("vector", f_pr, inc="dve")
                    em.wait("scalar", "dve", d_pr)
                    def f_lnM(s):
                        return s.activation(srow[0:1, 523:524], srow[0:1, 522:523], AF.Ln)
                    a_lnM = em.op("scalar", f_lnM, inc="act")
                    em.wait("scalar", "act", a_lnM)
                    def f_exM(s):
                        return s.activation(srow[0:1, 524:525], srow[0:1, 523:524],
                                            AF.Exp, scale=0.5)
                    a_M = em.op("scalar", f_exM, inc="act")
                    em.wait("tensor", "act", a_M)
                    em.wait("tensor", "dve", st8.get(("ef_war", False), 0))
                    def f_bc(t):
                        return t.matmul(psF[:, 0:1], ones_row[0:1, :], srow[0:1, 524:525],
                                        start=True, stop=True)
                    p_bc = em.op("tensor", f_bc, inc="pe")
                    em.wait("vector", "pe", p_bc)
                    def f_bias(v, slot=slot):
                        return v.tensor_scalar(bias_h[:, slot:slot + 1], psF[:, 0:1],
                                               -INV_SQRT_DH, None, ALU.mult)
                    d_bias = em.op("vector", f_bias, inc="dve")
                    st8[("ef_war", False)] = d_bias
                    st8[("bias", h)] = d_bias
                steps.append(finale)
                return steps

            load_head(0)
            for s in production_steps(0):
                s()
            load_head(1)
            for h in range(HL):
                slot = h % 2
                kqt, _ = kq_base(h)
                pending = production_steps(h + 1)
                if h + 2 < HL:
                    pass
                em.wait("scalar", "dve", st8[("bias", h)])
                em.wait("tensor", "dve", st8[("kq_ev", h)])
                exp_last = st8.setdefault("exp_last", {})
                for qc in range(4):
                    nkc = 4 * qc + 4
                    exp_done = {}
                    u_ready = {}
                    av_done = {}
                    sbanks = (psA, psB, psE)

                    def emit_score(kc):
                        sbank = sbanks[kc % 3]
                        em.wait("tensor", "act",
                                exp_done.get(kc - 3, 0) or exp_last.get(kc % 3, 0))
                        def f_sc(t, kc=kc, sbank=sbank, qc=qc, kqt=kqt):
                            return t.matmul(sbank[:, :],
                                            kqt[:, kc * 128:(kc + 1) * 128],
                                            kqt[:, 2048 + qc * 512: 2048 + (qc + 1) * 512],
                                            start=True, stop=True)
                        p_sc = em.op("tensor", f_sc, inc="pe")
                        em.wait("scalar", "pe", p_sc)
                        em.wait("scalar", "pe", av_done.get(kc - 3, 0))
                        def f_exp(s, kc=kc, sbank=sbank, slot=slot):
                            return s.activation(u_sb[:, (kc % 3) * 512:(kc % 3 + 1) * 512],
                                                sbank[:, :], AF.Exp,
                                                bias=bias_h[:, slot:slot + 1],
                                                scale=INV_SQRT_DH)
                        exp_done[kc] = em.op("scalar", f_exp, inc="act")
                        exp_last[kc % 3] = exp_done[kc]
                        j = kc - 4 * qc
                        if j >= 0:
                            em.wait("vector", "act", exp_done[kc])
                            def f_mask(v, kc=kc, j=j):
                                sl = u_sb[:, (kc % 3) * 512:(kc % 3 + 1) * 512]
                                return v.tensor_mul(sl, sl, masks[:, j * 512:(j + 1) * 512])
                            u_ready[kc] = ("dve", em.op("vector", f_mask, inc="dve"))
                        else:
                            u_ready[kc] = ("act", exp_done[kc])

                    def emit_sumav(kc):
                        eng, val = u_ready[kc]
                        em.wait("tensor", eng, val)
                        if kc == 0:
                            em.wait("tensor", "dve", st8.get("d_inv", 0))
                            em.wait("tensor", "dve", st8.get("z_rd", 0))
                        def f_sum(t, kc=kc, nkc=nkc):
                            return t.matmul(psD[:, :], ones128[:, :],
                                            u_sb[:, (kc % 3) * 512:(kc % 3 + 1) * 512],
                                            start=(kc == 0), stop=(kc == nkc - 1))
                        em.op("tensor", f_sum)
                        def f_av(t, kc=kc, h=h, nkc=nkc):
                            return t.matmul(psC[:, :],
                                            v_sb[:, kc * 1024 + h * 128: kc * 1024 + (h + 1) * 128],
                                            u_sb[:, (kc % 3) * 512:(kc % 3 + 1) * 512],
                                            start=(kc == 0), stop=(kc == nkc - 1))
                        av_done[kc] = em.op("tensor", f_av, inc="pe")

                    for kc in range(nkc):
                        emit_score(kc)
                        if kc >= 2:
                            emit_sumav(kc - 2)
                        if kc % 2 == 0 and pending:
                            pending.pop(0)()
                        elif not DBG_NO_QUANT:
                            quant_pump(1)
                        if kc % 2 == 1 and not pending and not DBG_NO_QUANT:
                            quant_pump(1)
                    emit_sumav(nkc - 2)
                    emit_sumav(nkc - 1)
                    p_att = av_done[nkc - 1]
                    em.wait("vector", "pe", p_att)
                    def f_inv(v):
                        return v.reciprocal(invs_sb[:, :], psD[:, :])
                    st8["d_inv"] = em.op("vector", f_inv, inc="dve")
                    def f_z(v, h=h, qc=qc):
                        return v.tensor_mul(
                            z_sb[:, h * 2048 + qc * 512: h * 2048 + (qc + 1) * 512],
                            psC[:, :], invs_sb[:, :])
                    st8["z_rd"] = em.op("vector", f_z, inc="dve")
                    if not DBG_NO_QUANT:
                        quant_pump(1)

                st8[("att_done", h)] = em.cnt["pe"]
                while pending:
                    pending.pop(0)()
                if h + 2 < HL:
                    load_head(h + 2)
            sb_close("kq2")
            sb_close("sqh")
            sb_close("wqkbuf")

        def emit_wo_loads():
            wo_sb = sb_open("wo_sb", [128, HL * 2048], BF16)
            em.wait("gpsimd", "pe", st8[("qk_pe", HL - 1)])
            last_ld = 0
            for fc in range(HL):
                def f(g, fc=fc):
                    return g.dma_start(wo_sb[:, fc * 2048:(fc + 1) * 2048], wo[fc][:, :])
                last_ld = em.op("gpsimd", f, inc="dgp", amt=16)
            return wo_sb, last_ld

        def emit_wo(wo_sb, last_ld):
            wo_stage = sb_open("wo_stage", [128, 2 * 2048], F32)
            stg = {}
            store = {}
            em.wait("tensor", "dgp", last_ld)
            order = [0, 1, 2, 3, 8, 9, 10, 11, 4, 5, 6, 7, 12, 13, 14, 15]
            for idx, tt in enumerate(order):
                slot = idx % 2
                stt_ = wo_stage[:, slot * 2048:(slot + 1) * 2048]
                for dc in range(4):
                    bank = (psA, psB)[dc % 2]
                    em.wait("tensor", "dve",
                            stg.get((idx, dc - 2), 0) or stg.get((idx - 1, dc + 2), 0))
                    p_wo = None
                    for fc in range(HL):
                        def f_mm(t, tt=tt, dc=dc, fc=fc, bank=bank):
                            return t.matmul(
                                bank[:, :],
                                z_sb[:, fc * 2048 + tt * 128: fc * 2048 + (tt + 1) * 128],
                                wo_sb[:, fc * 2048 + dc * 512: fc * 2048 + (dc + 1) * 512],
                                start=(fc == 0), stop=(fc == HL - 1))
                        p_wo = em.op("tensor", f_mm, inc=("pe" if fc == HL - 1 else None))
                    em.wait("vector", "pe", p_wo)
                    em.wait("vector", "dwo%d" % slot, store.get(idx - 2, 0))
                    def f_ev(v, dc=dc, stt_=stt_, bank=bank):
                        return v.tensor_copy(stt_[:, dc * 512:(dc + 1) * 512], bank[:, :])
                    stg[(idx, dc)] = em.op("vector", f_ev, inc="dve")
                em.wait("sync", "dve", stg[(idx, 3)])
                rank, q = tt // 8, tt % 8
                pos = (q // 4) * 1024 + rank * 512 + (q % 4) * 128
                def f_st(sy, pos=pos, stt_=stt_):
                    return sy.dma_start(wo_part[pos:pos + 128, :], stt_)
                store[idx] = em.op("sync", f_st, inc="dwo%d" % slot, amt=16)
                if idx % 8 == 7:
                    em.wait("gpsimd", "dwo0", store[idx - 1])
                    em.wait("gpsimd", "dwo1", store[idx])
                    c = idx // 8
                    def f_rs(g, c=c):
                        return g.collective_compute(
                            "ReduceScatter", ALU.add,
                            replica_groups=[[0, 1], [2, 3], [4, 5], [6, 7]],
                            ins=[wo_part[c * 1024:(c + 1) * 1024, :].opt()],
                            outs=[attn_out[c * 512:(c + 1) * 512, :].opt()])
                    em.op("gpsimd", f_rs, inc="ccrs")
            sb_close("wo_stage")

        def emit_ln2():
            xn2T = sb_open("xn2T", [128, 16 * 1024], BF16, side="right")
            abuf = sb_open("abuf", [128, 2 * 2048], F32)
            rbuf = sb_open("rbuf", [128, 2 * 2048], F32)
            sqx2 = sb_open("sqx2", [128, 2048], F32)
            nb2 = sb_open("nb2", [128, 2 * 2048], BF16)
            ln_done = {}
            trpe = {}
            for tl in range(8):
                em.wait("sync", "ccrs", tl // 4 + 1)
                slot = tl % 2
                at = abuf[:, slot * 2048:(slot + 1) * 2048]
                rt = rbuf[:, slot * 2048:(slot + 1) * 2048]
                em.wait("sync", "dve", ln_done.get(tl - 2, 0))
                def f_a(sy, tl=tl, at=at):
                    return sy.dma_start(at, attn_out[tl * 128:(tl + 1) * 128, :])
                em.op("sync", f_a, inc="dsy", amt=16)
                def f_r(sy, tl=tl, rt=rt):
                    return sy.dma_start(rt, x_res[tl * 128:(tl + 1) * 128, :])
                d_r = em.op("sync", f_r, inc="dsy", amt=16)
                em.wait("vector", "dsy", d_r)
                def f_add(v, at=at, rt=rt):
                    return v.tensor_add(rt, rt, at)
                d_x1 = em.op("vector", f_add, inc="dve")
                em.wait("vector", "dve", d_x1)
                em.wait("sync", "dve", d_x1)
                def f_st(sy, tl=tl, rt=rt):
                    return sy.dma_start(x1d[tl * 128:(tl + 1) * 128, :], rt)
                em.op("sync", f_st, inc="dsy", amt=16)
                nb = nb2[:, slot * 2048:(slot + 1) * 2048]
                d_n = emit_ln(rt, nb, sqx2[:, :], pe_war=trpe.get(tl - 2, 0))
                ln_done[tl] = d_n
                em.wait("tensor", "dve", d_n)
                _, p_tr = emit_transpose_block(nb, xn2T, tl * 128, 1024, 16)
                trpe[tl] = p_tr
            sb_close("nb2")
            sb_close("sqx2")
            sb_close("rbuf")
            sb_close("abuf")
            return xn2T

        def emit_ffn(xn2T):
            h_sb = sb_open("h_sb", [128, 64 * 512], BF16)
            wgu_t = sb_open("wgu_t", [128, 2 * 2 * 2048], BF16)
            sg_buf = sb_open("sg_buf", [128, 2 * 512], F32)
            h2_buf = sb_open("h2_buf", [128, 2 * 512], F32)
            wp_t = sb_open("wp_t", [128, 2 * 8 * 512], BF16)
            x1buf = sb_open("x1buf", [128, 4 * 2048], F32)

            em.wait("sync", "ccgu", em.cnt["ccgu"])
            em.wait("sync", "ccp", em.cnt["ccp"])

            def load_gu(gpc):
                th2, pc = divmod(gpc, 64)
                slot = gpc % 2
                base = slot * 2 * 2048
                em.wait("sync", "pe", st8.get(("gu_pe", gpc - 2), 0))
                last = 0
                for gi, hc in ((0, pc), (1, pc + 64)):
                    o_ap = bass.AP(wgu_t, base + gi * 2048,
                                   [[2 * 2 * 2048, 128], [128, 16], [1, 128]])
                    i_ap = bass.AP(cc_gu_out, hc * 2048 * 128,
                                   [[2048, 128], [128, 16], [1, 128]])
                    def f(sy, o_ap=o_ap, i_ap=i_ap):
                        return sy.dma_start(o_ap, i_ap)
                    last = em.op("sync", f, inc="dgu%d" % slot, amt=16)
                st8[("gu_ld", gpc)] = last

            def load_wp(th2, dc, r, slot):
                em.wait("sync", "pe", st8.get(("wp_war", slot), 0))
                o_ap = bass.AP(wp_t, slot * 8 * 512,
                               [[2 * 8 * 512, 128], [512, 8], [1, 512]])
                i_ap = bass.AP(cc_p_out, (r * 4096 + dc * 1024) * 512,
                               [[512, 128], [128 * 512, 8], [1, 512]])
                def f(sy, o_ap=o_ap, i_ap=i_ap):
                    return sy.dma_start(o_ap, i_ap)
                return em.op("sync", f, inc="dwp%d" % slot, amt=16)

            load_gu(0)
            for th2 in range(2):
                def emit_ssq(pc):
                    em.wait("tensor", "dve", st8[("h2", pc)])
                    if pc == 0:
                        em.wait("tensor", "dve", st8.get("rstd_rd", 0))
                    def f_ssq(t, pc=pc):
                        return t.matmul(psE[:, :], ones128f[:, :],
                                        h2_buf[:, (pc % 2) * 512:(pc % 2 + 1) * 512],
                                        start=(pc == 0), stop=(pc == 63))
                    st8[("ssq_pe", pc)] = em.op("tensor", f_ssq, inc="pe")

                for pc in range(64):
                    gpc = th2 * 64 + pc
                    slot = gpc % 2
                    base = slot * 2 * 2048
                    if pc == 0:
                        for j in range(4):
                            em.wait("tensor", "dve", st8.get(("prj_war", j), 0))
                    if gpc + 1 < 128:
                        load_gu(gpc + 1)
                    em.wait("tensor", "dgu%d" % (gpc % 2), st8[("gu_ld", gpc)])
                    gbank = (psA, psB)[gpc % 2]
                    ubank = (psC, psD)[gpc % 2]
                    em.wait("tensor", "act", st8.get(("silu_war", gbank is psA), 0))
                    em.wait("tensor", "dve", st8.get(("h_war", ubank is psC), 0))
                    p_g = None
                    for dc in range(16):
                        def f_g(t, dc=dc, th2=th2, gbank=gbank, base=base):
                            return t.matmul(
                                gbank[:, :],
                                wgu_t[:, base + dc * 128: base + (dc + 1) * 128],
                                xn2T[:, dc * 1024 + th2 * 512: dc * 1024 + (th2 + 1) * 512],
                                start=(dc == 0), stop=(dc == 15))
                        p_g = em.op("tensor", f_g, inc=("pe" if dc == 15 else None))
                    p_u = None
                    for dc in range(16):
                        def f_u(t, dc=dc, th2=th2, ubank=ubank, base=base):
                            return t.matmul(
                                ubank[:, :],
                                wgu_t[:, base + 2048 + dc * 128: base + 2048 + (dc + 1) * 128],
                                xn2T[:, dc * 1024 + th2 * 512: dc * 1024 + (th2 + 1) * 512],
                                start=(dc == 0), stop=(dc == 15))
                        p_u = em.op("tensor", f_u, inc=("pe" if dc == 15 else None))
                    st8[("gu_pe", gpc)] = p_u
                    if pc > 0:
                        emit_ssq(pc - 1)
                    em.wait("scalar", "pe", p_g)
                    em.wait("scalar", "dve", st8.get(("sg_war", pc % 2), 0))
                    def f_silu(s, gbank=gbank, pc=pc):
                        return s.activation(sg_buf[:, (pc % 2) * 512:(pc % 2 + 1) * 512],
                                            gbank[:, :], AF.Silu)
                    a_silu = em.op("scalar", f_silu, inc="act")
                    st8[("silu_war", gbank is psA)] = a_silu
                    em.wait("vector", "act", a_silu)
                    em.wait("vector", "pe", p_u)
                    em.wait("vector", "pe", st8.get("proj_done", 0))
                    hsl = h_sb[:, pc * 512:(pc + 1) * 512]
                    def f_h(v, hsl=hsl, ubank=ubank, pc=pc):
                        return v.tensor_mul(hsl, sg_buf[:, (pc % 2) * 512:(pc % 2 + 1) * 512],
                                            ubank[:, :])
                    d_h = em.op("vector", f_h, inc="dve")
                    st8[("h_war", ubank is psC)] = d_h
                    st8[("sg_war", pc % 2)] = d_h
                    em.wait("vector", "pe", st8.get(("ssq_pe", pc - 2), 0))
                    def f_h2(v, hsl=hsl, pc=pc):
                        return v.tensor_mul(h2_buf[:, (pc % 2) * 512:(pc % 2 + 1) * 512],
                                            hsl, hsl)
                    st8[("h2", pc)] = em.op("vector", f_h2, inc="dve")
                emit_ssq(63)
                em.wait("scalar", "pe", st8[("ssq_pe", 63)])
                def f_lnS(s):
                    return s.activation(srow[0:1, 0:512], psE[0:1, :], AF.Ln,
                                        bias=epsb[0:1, 1:2], scale=1.0 / HID)
                a_lnS = em.op("scalar", f_lnS, inc="act")
                em.wait("scalar", "act", a_lnS)
                def f_exS(s):
                    return s.activation(srow[0:1, 512:1024], srow[0:1, 0:512],
                                        AF.Exp, scale=-0.5)
                a_r = em.op("scalar", f_exS, inc="act")
                em.wait("tensor", "act", a_r)
                for j in range(4):
                    def f_rc(t, j=j):
                        return t.matmul(psE[:, j:j + 1],
                                        srow[0:1, 512 + j * 128: 512 + (j + 1) * 128],
                                        ones_row[0:1, 0:1], start=True, stop=True)
                    p_rc = em.op("tensor", f_rc, inc=("pe" if j == 3 else None))
                em.wait("vector", "pe", p_rc)
                def f_evR(v):
                    return v.tensor_copy(stats[:, 8:12], psE[:, 0:4])
                d_evR = em.op("vector", f_evR, inc="dve")
                st8["rstd_rd"] = d_evR
                x1_ld = {}
                for j in range(4):
                    tt = th2 * 4 + j
                    em.wait("sync", "dve", st8.get(("x1_rd", j), 0))
                    def f_x1(sy, tt=tt, j=j):
                        return sy.dma_start(x1buf[:, j * 2048:(j + 1) * 2048],
                                            x1d[tt * 128:(tt + 1) * 128, :])
                    x1_ld[j] = em.op("sync", f_x1, inc="dsy", amt=16)
                em.wait("tensor", "dve", st8[("h2", 63)])
                banks4 = (psA, psB, psC, psD)
                for dc in range(4):
                    wp_ld = {0: load_wp(th2, dc, 0, 0)}
                    p_r = 0
                    for r in range(8):
                        slot = r % 2
                        if r + 1 < 8:
                            wp_ld[r + 1] = load_wp(th2, dc, r + 1, (r + 1) % 2)
                        em.wait("tensor", "dwp%d" % (r % 2), wp_ld[r])
                        for rc in range(8):
                            hc = r * 8 + rc
                            for j in range(4):
                                if hc == 0:
                                    em.wait("tensor", "dve",
                                            st8.get(("prj_war", j), 0))
                                def f_mm(t, hc=hc, j=j, rc=rc, slot=slot, th2=th2):
                                    tl = th2 * 4 + j - th2 * 4
                                    return t.matmul(
                                        banks4[j][:, :],
                                        h_sb[:, hc * 512 + j * 128: hc * 512 + (j + 1) * 128],
                                        wp_t[:, slot * 8 * 512 + rc * 512:
                                             slot * 8 * 512 + (rc + 1) * 512],
                                        start=(hc == 0), stop=(hc == 63))
                                inc = "pe" if (rc == 7 and j == 3) else None
                                p_r = em.op("tensor", f_mm, inc=inc)
                        st8[("wp_war", slot)] = p_r
                    for j in range(4):
                        em.wait("vector", "pe", p_r)
                        em.wait("vector", "dsy", x1_ld[3])
                        em.wait("vector", "dve", st8["rstd_rd"])
                        def f_ev(v, j=j, dc=dc):
                            sl = x1buf[:, j * 2048 + dc * 512: j * 2048 + (dc + 1) * 512]
                            return v.scalar_tensor_tensor(sl, banks4[j][:, :],
                                                          stats[:, 8 + j:9 + j], sl,
                                                          ALU.mult, ALU.add)
                        st8[("prj_war", j)] = em.op("vector", f_ev, inc="dve")
                for j in range(4):
                    tt = th2 * 4 + j
                    em.wait("sync", "dve", st8[("prj_war", j)])
                    st8[("x1_rd", j)] = st8[("prj_war", j)]
                    def f_out(sy, tt=tt, j=j):
                        return sy.dma_start(out_t[tt * 128:(tt + 1) * 128, :],
                                            x1buf[:, j * 2048:(j + 1) * 2048])
                    em.op("sync", f_out, inc="dsy", amt=16)
                st8["proj_done"] = st8.get(("wp_war", 0), 0) or 0
                st8["proj_done"] = max(st8.get("proj_done", 0),
                                       st8.get(("wp_war", 1), 0) or 0)

        emit_consts()
        emit_ln1_v()
        sb_close("xnbuf")
        sb_close("wv_sb")
        sb_close("sqx")
        sb_close("xin")
        if stage in ("heads", "wo", "ln2", "full"):
            z_sb = sb_open("z_sb", [128, HL * 2048], BF16, side="right")
            qwbf = sb_open("qwbf", [128, 2 * 2048], BF16, side="right")
            qabs = sb_open("qabs", [128, 2 * 2048], BF16, side="right")
            qsgn = sb_open("qsgn", [128, 2 * 2048], BF16, side="right")
            qg = sb_open("qg", [128, 2 * 2048], BF16, side="right")
            qsbf = sb_open("qsbf", [128, 512], BF16, side="right")
            em.fence()
            emit_heads()
            sb_close("xn1T")
            while qs["done"] < len(q_tiles):
                quant_step()
            quant_gathers()
            wo_sb, wo_last_ld = emit_wo_loads()
        if stage in ("wo", "ln2", "full"):
            em.fence()
            emit_wo(wo_sb, wo_last_ld)
            for nm in ("qsbf", "qg", "qsgn", "qabs", "qwbf"):
                sb_close(nm)
            sb_close("wo_sb")
            sb_close("v_sb")
            sb_close("z_sb")
        xn2T = None
        if stage in ("ln2", "full"):
            em.fence()
            xn2T = emit_ln2()
        if stage == "full":
            em.fence()
            emit_ffn(xn2T)
            for nm in ("x1buf", "wp_t", "h2_buf", "sg_buf", "wgu_t", "h_sb"):
                sb_close(nm)
            sb_close("xn2T")

        tap_outs = []
        for name, key, ncols in taps:
            tensor = reg[key]
            o = nc.dram_tensor("tap_" + name, [128, ncols], tensor.dtype,
                               kind="ExternalOutput")
            tap_outs.append((o, tensor, ncols))
        if tap_outs:
            em.fence(("sync",), exclude=())
            for o, tensor, ncols in tap_outs:
                def f(sy, o=o, tensor=tensor, ncols=ncols):
                    return sy.dma_start(o[:, :], tensor[:, 0:ncols])
                em.op("sync", f, inc="dsy", amt=16)

        def f_fin(sy):
            return sy.wait_ge(sems["dsy"], em.cnt["dsy"])
        em.q["sync"].append(f_fin)
        def f_fin2(g):
            return g.wait_ge(sems["dgp"], em.cnt["dgp"])
        em.q["gpsimd"].append(f_fin2)

        with nc.Block() as block:
            @block.tensor
            def _(t):
                for f in em.q["tensor"]:
                    f(t)

            @block.vector
            def _(v):
                for f in em.q["vector"]:
                    f(v)

            @block.scalar
            def _(s):
                for f in em.q["scalar"]:
                    f(s)

            @block.gpsimd
            def _(g):
                for f in em.q["gpsimd"]:
                    f(g)

            @block.sync
            def _(sy):
                for f in em.q["sync"]:
                    f(sy)

        for name in list(reversed(list(_open_cms))):
            sb_close(name)

    return nc


def make_inputs(L0_attn_x, L0_attn_W_K, L0_attn_W_Q, L0_attn_W_V, L0_attn_W_O,
                L0_ffn_w_gate_up, L0_ffn_w_proj):
    x = np.asarray(L0_attn_x, np.float32)
    W_K = np.asarray(L0_attn_W_K, np.float32)
    W_Q = np.asarray(L0_attn_W_Q, np.float32)
    W_V = np.asarray(L0_attn_W_V, np.float32)
    W_O = np.asarray(L0_attn_W_O, np.float32)
    wgu_f = np.asarray(L0_ffn_w_gate_up, np.float32)
    wp_f = np.asarray(L0_ffn_w_proj, np.float32)
    woT = np.ascontiguousarray(W_O.T)
    in_maps = []
    for c in range(NCORES):
        b, par = c // 2, c % 2
        hr = slice(par * HL, (par + 1) * HL)
        wq_t = W_Q[hr].transpose(0, 2, 1).reshape(HL, 16, 128, 128)
        wk_t = W_K[hr].transpose(0, 2, 1).reshape(HL, 16, 128, 128)
        wqk_h = np.ascontiguousarray(np.stack([wq_t, wk_t]))
        wv_t = np.ascontiguousarray(
            W_V[hr].transpose(0, 2, 1).reshape(HL, 16, 128, 128)
            .transpose(1, 2, 0, 3)).reshape(16, 128, HL * 128)
        wo_my = np.ascontiguousarray(woT[par * 1024:(par + 1) * 1024]).reshape(HL, 128, D)
        wgu_sh = np.ascontiguousarray(wgu_f[c * 2048:(c + 1) * 2048, :].T).reshape(16, 128, 2048)
        wp_sh = np.ascontiguousarray(wp_f[:, c * 1024:(c + 1) * 1024].T).reshape(8, 128, 2048)
        in_maps.append({
            "x_full": np.ascontiguousarray(x[b]),
            "x_res": np.ascontiguousarray(x[b, par * TOK:(par + 1) * TOK]),
            "wqk": wqk_h,
            "wv": wv_t,
            "wo": wo_my,
            "wgu": wgu_sh,
            "wp": wp_sh,
        })
    return in_maps


def kernel(L0_attn_x, L0_attn_W_K, L0_attn_W_Q, L0_attn_W_V, L0_attn_W_O,
           L0_ffn_w_gate_up, L0_ffn_w_proj):
    in_maps = make_inputs(L0_attn_x, L0_attn_W_K, L0_attn_W_Q, L0_attn_W_V,
                          L0_attn_W_O, L0_ffn_w_gate_up, L0_ffn_w_proj)
    nc = build("full")
    res = run_bass_kernel_spmd(nc, in_maps, core_ids=list(range(NCORES)))
    out = np.empty((B, S, D), np.float32)
    for c in range(NCORES):
        b, par = c // 2, c % 2
        out[b, par * TOK:(par + 1) * TOK] = res.results[c]["out"]
    return out


def build_with_taps(stage, taps, **kw):
    return build(stage, taps)
